# version 16
# baseline (speedup 1.0000x reference)
import numpy as np
import ml_dtypes
import concourse.bass as bass
import concourse.mybir as mybir
from concourse.bass_utils import run_bass_kernel_spmd
from contextlib import ExitStack

F32 = mybir.dt.float32
BF16 = mybir.dt.bfloat16
AF = mybir.ActivationFunctionType
ALU = mybir.AluOpType
AX = mybir.AxisListType

D = 2048
DC = 16
NT = 1024
HALO = 512
NPAIR = 8
NKV = 4
NSLOT = 12
HC = 64
LN_EPS = 1e-5
RMS_EPS = float(np.finfo(np.float32).eps)
NEG = -1e30

PHASES = (1, 2, 3, 4)
TAP_I, TAP_QI = 1, 0
SERIAL_SCORES = False
DEBUG_TAPS = False
LAST_RES = None
BLOCKNAME = {"pe": "tensor", "dv": "vector", "ac": "scalar", "sy": "sync", "pl": "gpsimd"}
DMA_RR = {"sy": 2, "pl": 4}


class Prog:

    def __init__(self):
        self.items = []
        self.bufs = {}
        self.bar = None
        self.bar_taken = set()

    def op(self, eng, emit, reads=(), writes=(), dma=False, sem_reads=(), nopinc=False):
        opid = len(self.items)
        deps = set()

        def want(dep_opid, force=False):
            if dep_opid is None:
                return
            d = self.items[dep_opid]
            if not force and d["eng"] == eng and not d["dma"] and not dma:
                return
            deps.add(dep_opid)

        for b in sem_reads:
            e = self.bufs.get(b)
            if e is not None:
                want(e["w"], force=True)
        for b in reads:
            e = self.bufs.get(b)
            if e is not None:
                want(e["w"])
        for b in writes:
            e = self.bufs.get(b)
            if e is not None:
                want(e["w"], force=True)
                for r in e["r"].values():
                    want(r, force=True)
        if self.bar is not None and eng not in self.bar_taken:
            for oid in self.bar.values():
                want(oid)
            self.bar_taken.add(eng)
        self.items.append(dict(opid=opid, eng=eng, emit=emit, deps=deps, dma=dma,
                               nopinc=nopinc))
        rtag = "dma%d" % opid if dma else eng
        for b in list(sem_reads):
            self.bufs.setdefault(b, {"w": None, "r": {}})["r"][rtag] = opid
        for b in reads:
            self.bufs.setdefault(b, {"w": None, "r": {}})["r"][rtag] = opid
        for b in writes:
            self.bufs[b] = {"w": opid, "r": {}}
        return opid

    def barrier(self):
        last = {}
        for it in self.items:
            last[it["eng"]] = it["opid"]
        self.bar = last
        self.bar_taken = set()

    def finalize_and_emit(self, nc, blk, st):
        rr_cnt = {e: 0 for e in DMA_RR}
        for it in self.items:
            if it["dma"]:
                it["sem"] = it["eng"] + str(rr_cnt[it["eng"]] % DMA_RR[it["eng"]])
                rr_cnt[it["eng"]] += 1
            else:
                it["sem"] = it["eng"]
        milestones = {it["opid"] for it in self.items if it["dma"]}
        for it in self.items:
            milestones |= it["deps"]
        sem_names = sorted({self.items[m]["sem"] for m in milestones})
        sems = {n: st.enter_context(nc.semaphore("sem_" + n)) for n in sem_names}
        cum = {n: 0 for n in sem_names}
        val = {}
        for it in self.items:
            if it["opid"] in milestones:
                cum[it["sem"]] += 16 if it["dma"] else 1
                val[it["opid"]] = (it["sem"], cum[it["sem"]])
        per_eng = {}
        for it in self.items:
            per_eng.setdefault(it["eng"], []).append(it)

        def make_body(items):
            def body(e):
                waited = {}
                for it in items:
                    wl = {}
                    for dep in it["deps"]:
                        sname, v = val[dep]
                        if wl.get(sname, 0) < v:
                            wl[sname] = v
                    for sname in sorted(wl):
                        v = wl[sname]
                        if waited.get(sname, 0) < v:
                            e.wait_ge(sems[sname], v)
                            waited[sname] = v
                    inst = it["emit"](e)
                    if it["opid"] in milestones:
                        if it["dma"]:
                            inst.then_inc(sems[it["sem"]], 16)
                        elif it.get("nopinc"):
                            e.maybe_nop_then_inc((sems[it["sem"]], 1))
                        else:
                            e.maybe_drain_then_inc((sems[it["sem"]], 1), fusable=True)
            return body

        for eng, items in per_eng.items():
            getattr(blk, BLOCKNAME[eng])(make_body(items))


def build_program(phases=(1, 2, 3, 4)):
    nc = bass.Bass("TRN2", target_bir_lowering=False)
    dX = nc.dram_tensor("xo", [128, DC, NT], F32, kind="ExternalInput").ap()
    dXh = nc.dram_tensor("xh", [128, DC, HALO], F32, kind="ExternalInput").ap()
    dCos = nc.dram_tensor("cosb", [128, HALO + NT], BF16, kind="ExternalInput").ap()
    dSin = nc.dram_tensor("sinb", [128, HALO + NT], BF16, kind="ExternalInput").ap()
    dAmask = nc.dram_tensor("amask", [128, 8, 640], BF16, kind="ExternalInput").ap()
    dWq = nc.dram_tensor("wq_t", [16, 128, D], F32, kind="ExternalInput").ap()
    dWk = nc.dram_tensor("wk_t", [4, 128, D], F32, kind="ExternalInput").ap()
    dWv = nc.dram_tensor("wv", [D, 512], F32, kind="ExternalInput").ap()
    dWo = nc.dram_tensor("wo_t", [16, 128, D], F32, kind="ExternalInput").ap()
    dLam = nc.dram_tensor("lam_bc", [128, 16], F32, kind="ExternalInput").ap()
    dWqb = nc.dram_tensor("wqb_t", [16, 128, D], F32, kind="ExternalInput").ap()
    dWout = nc.dram_tensor("wout_t", [16, 128, D], F32, kind="ExternalInput").ap()
    dPmk = nc.dram_tensor("pmk_bd", [128, 16, 32], F32, kind="ExternalInput").ap()
    dPmv = nc.dram_tensor("pmv_bd", [32, 16, 128], F32, kind="ExternalInput").ap()
    dBmask = nc.dram_tensor("bmask", [128, 512], F32, kind="ExternalInput").ap()
    dWn = nc.dram_tensor("wn_fm", [128, DC], F32, kind="ExternalInput").ap()
    dWg = nc.dram_tensor("wg_t", [HC, 128, D], F32, kind="ExternalInput").ap()
    dWu = nc.dram_tensor("wu_t", [HC, 128, D], F32, kind="ExternalInput").ap()
    dWd = nc.dram_tensor("wd_t", [16, 128, 8192], F32, kind="ExternalInput").ap()
    dIdent = nc.dram_tensor("ident", [128, 128], BF16, kind="ExternalInput").ap()
    dY = nc.dram_tensor("y", [128, DC, NT], F32, kind="ExternalOutput").ap()
    dXspill = nc.dram_tensor("xspill", [128, DC, NT], F32, kind="Internal").ap()
    dWgS = nc.dram_tensor("wg_s", [HC, 128, D], BF16, kind="Internal").ap()
    dWuS = nc.dram_tensor("wu_s", [HC, 128, D], BF16, kind="Internal").ap()
    dWdS = nc.dram_tensor("wd_s", [16, 128, 8192], BF16, kind="Internal").ap()

    p = Prog()
    st = ExitStack()
    with st:
        xs = st.enter_context(nc.sbuf_tensor("xs", [128, DC, NT], F32))
        ident = st.enter_context(nc.sbuf_tensor("identb", [128, 128], BF16))
        ones_f = st.enter_context(nc.sbuf_tensor("ones_f", [128, 1], F32))
        ones_b = st.enter_context(nc.sbuf_tensor("ones_b", [128, 1], BF16))
        ones_r = st.enter_context(nc.sbuf_tensor("ones_r", [1, 128], F32))
        sqb = st.enter_context(nc.sbuf_tensor("sqb", [128, 2, 512], BF16))
        strow = st.enter_context(nc.sbuf_tensor("strow", [1, 3, 512], F32))
        ab_row = st.enter_context(nc.sbuf_tensor("ab_row", [1, 1024], F32))
        lamn = st.enter_context(nc.sbuf_tensor("lamn", [128, 16], F32))
        wnfm = st.enter_context(nc.sbuf_tensor("wnfm", [128, DC], F32))
        lam_t = st.enter_context(nc.sbuf_tensor("lam_t", [128, 16], F32))
        lntmp = st.enter_context(nc.sbuf_tensor("lntmp", [128, 2, 512], F32))
        epsv = st.enter_context(nc.sbuf_tensor("epsv", [128, 2], F32))

        def dma(eng, dst, src, reads=(), writes=()):
            return p.op(eng, (lambda e, d=dst, s=src: e.dma_start(out=d, in_=s)),
                        reads=reads, writes=writes, dma=True)

        dma("sy", ident[:], dIdent, writes=["ident"])
        dma("sy", wnfm[:], dWn, writes=["wnfm"])
        dma("sy", lam_t[:], dLam, writes=["lam"])
        p.op("dv", lambda e: e.tensor_scalar(out=lamn[:], in0=lam_t[:], scalar1=-1.0,
                                             scalar2=None, op0=ALU.mult),
             reads=["lam"], writes=["lamn"])
        p.op("dv", lambda e: e.memset(ones_f[:], 1.0), writes=["ones_f"])
        p.op("dv", lambda e: e.memset(ones_b[:], 1.0), writes=["ones_b"])
        p.op("dv", lambda e: e.memset(ones_r[:], 1.0), writes=["ones_r"])
        p.op("dv", lambda e: e.memset(epsv[:, 0:1], LN_EPS), writes=["eps0"])
        p.op("dv", lambda e: e.memset(epsv[:, 1:2], RMS_EPS), writes=["eps1"])
        for h in range(2):
            dma("sy", xs[:, :, h * 512:(h + 1) * 512], dX[:, :, h * 512:(h + 1) * 512],
                writes=[f"x{dc}c{h}" for dc in range(DC)])

        def xkeys(ch):
            return [f"x{dc}c{ch}" for dc in range(DC)]

        def tap(name, ap, shape, dtype, keys):
            if not DEBUG_TAPS:
                return
            dt_ = nc.dram_tensor(name, shape, dtype, kind="ExternalOutput").ap()
            dma("sy", dt_, ap, reads=keys, writes=[name])
            p.op("sy", lambda e: e.nop(), reads=[name])

        def emit_ln(tag, src, srckeys, dst, dstkeys, mode="plain", W=512):
            with nc.psum_tensor("pstat_" + tag, [1, 1024], F32) as pst, \
                 nc.psum_tensor("pbc_" + tag, [128, 1024], F32) as pbc, \
                 nc.sbuf_tensor("sqf_" + tag, [128, DC, W], BF16) as sqf:
                sA = ab_row[:, 0:W]
                sB = ab_row[:, 512:512 + W]
                s0 = strow[:, 0, 0:W]
                s1_ = strow[:, 1, 0:W]
                s2 = strow[:, 2, 0:W]

                def mm_x(e):
                    r = None
                    for dc in range(DC):
                        r = e.matmul(out=pst[:, 0:W], lhsT=ones_f[:],
                                     rhs=src[:, dc, :],
                                     start=(dc == 0), stop=(dc == DC - 1))
                    return r
                p.op("pe", mm_x, reads=list(srckeys) + ["ones_f"], writes=["pstatx"])
                p.op("ac", lambda e: e.activation(out=sqf[:], in_=src, func=AF.Square),
                     reads=list(srckeys), writes=["sqf"])

                def mm_q(e):
                    r = None
                    for dc in range(DC):
                        r = e.matmul(out=pst[:, 512:512 + W], lhsT=ones_b[:],
                                     rhs=sqf[:, dc, :],
                                     start=(dc == 0), stop=(dc == DC - 1))
                    return r
                p.op("pe", mm_q, reads=["sqf", "ones_b"], writes=["pstatq"])
                p.op("dv", lambda e: e.tensor_scalar(out=s0, in0=pst[0:1, 0:W],
                                                     scalar1=1.0 / D, scalar2=None,
                                                     op0=ALU.mult),
                     reads=["pstatx"], writes=["st0"])
                p.op("dv", lambda e: e.tensor_scalar(out=s1_, in0=pst[0:1, 512:512 + W],
                                                     scalar1=1.0 / D, scalar2=None,
                                                     op0=ALU.mult),
                     reads=["pstatq"], writes=["st1"])
                p.op("dv", lambda e: e.tensor_tensor(out=s2, in0=s0, in1=s0, op=ALU.mult),
                     reads=["st0"], writes=["st2"])
                p.op("dv", lambda e: e.tensor_tensor(out=s1_, in0=s1_, in1=s2,
                                                     op=ALU.subtract),
                     reads=["st1", "st2"], writes=["st1"])
                p.op("ac", lambda e: e.activation(out=s2, in_=s1_, func=AF.Ln,
                                                  bias=epsv[0:1, 0:1]),
                     reads=["st1", "eps0"], writes=["st2"])
                p.op("ac", lambda e: e.activation(out=sA, in_=s2, func=AF.Exp, scale=-0.5),
                     reads=["st2"], writes=["stA"])
                if mode == "rms":
                    p.op("dv", lambda e: e.tensor_tensor(out=s2, in0=sA, in1=sA,
                                                         op=ALU.mult),
                         reads=["stA"], writes=["st2"])
                    p.op("dv", lambda e: e.tensor_tensor(out=s2, in0=s1_, in1=s2,
                                                         op=ALU.mult),
                         reads=["st1", "st2"], writes=["st2"])
                    p.op("ac", lambda e: e.activation(out=s2, in_=s2, func=AF.Ln,
                                                      bias=epsv[0:1, 1:2]),
                         reads=["st2", "eps1"], writes=["st2"])
                    p.op("ac", lambda e: e.activation(out=s2, in_=s2, func=AF.Exp,
                                                      scale=-0.5),
                         reads=["st2"], writes=["st2"])
                    p.op("dv", lambda e: e.tensor_tensor(out=sA, in0=sA, in1=s2,
                                                         op=ALU.mult),
                         reads=["stA", "st2"], writes=["stA"])
                p.op("dv", lambda e: e.scalar_tensor_tensor(out=sB, in0=s0, scalar=-1.0,
                                                            in1=sA, op0=ALU.mult,
                                                            op1=ALU.mult),
                     reads=["st0", "stA"], writes=["stB"])

                def bcmm(e):
                    e.matmul(out=pbc[:, 0:512], lhsT=ones_r[:], rhs=ab_row[:, 0:512],
                             start=True, stop=True)
                    return e.matmul(out=pbc[:, 512:1024], lhsT=ones_r[:],
                                    rhs=ab_row[:, 512:1024], start=True, stop=True)
                p.op("pe", bcmm, reads=["stA", "stB", "ones_r"], writes=["pbc"])
                abc = pbc[:, 0:W].unsqueeze(1).broadcast_to((128, 2, W))
                bbc = pbc[:, 512:512 + W].unsqueeze(1).broadcast_to((128, 2, W))
                wnb = wnfm[:].unsqueeze(-1).broadcast_to((128, DC, W))
                for d2 in range(DC // 2):
                    sl = slice(2 * d2, 2 * d2 + 2)
                    p.op("dv", (lambda e, sl=sl, abc=abc: e.tensor_tensor(
                        out=lntmp[:, :, 0:W], in0=src[:, sl, :], in1=abc, op=ALU.mult)),
                        reads=[srckeys[2 * d2], srckeys[2 * d2 + 1], "pbc"],
                        writes=["lt"])
                    if mode == "rms":
                        p.op("dv", (lambda e, sl=sl, bbc=bbc: e.tensor_tensor(
                            out=lntmp[:, :, 0:W], in0=lntmp[:, :, 0:W], in1=bbc,
                            op=ALU.add)),
                            reads=["lt", "pbc"], writes=["lt"])
                        p.op("dv", (lambda e, sl=sl, wnb=wnb: e.tensor_tensor(
                            out=dst[:, sl, :], in0=lntmp[:, :, 0:W],
                            in1=wnb[:, sl, :], op=ALU.mult)),
                            reads=["lt", "wnfm"],
                            writes=[dstkeys[2 * d2], dstkeys[2 * d2 + 1]])
                    else:
                        p.op("dv", (lambda e, sl=sl, bbc=bbc: e.tensor_tensor(
                            out=dst[:, sl, :], in0=lntmp[:, :, 0:W], in1=bbc,
                            op=ALU.add)),
                            reads=["lt", "pbc"],
                            writes=[dstkeys[2 * d2], dstkeys[2 * d2 + 1]])
            p.barrier()

        def emit_proj_residual(tag, dW, srcbuf, srckey_fn):
            with nc.sbuf_tensor("wpb_" + tag, [128, 2, DC, 128], BF16) as wpb, \
                 nc.psum_tensor("pso_" + tag, [128, 2, 512], F32) as pso:
                for j in range(16):
                    bi = j % 2
                    dma("pl", wpb[:, bi, :, :],
                        dW[j].rearrange("p (dc o) -> p dc o", o=128),
                        writes=[f"wpb{tag}{bi}"])
                    def omm(e, bi=bi):
                        r = None
                        for fc in range(DC):
                            for ch in range(2):
                                r = e.matmul(out=pso[:, ch, :], lhsT=wpb[:, bi, fc, :],
                                             rhs=srcbuf[:, fc, ch * 512:(ch + 1) * 512],
                                             start=(fc == 0), stop=(fc == DC - 1))
                        return r
                    p.op("pe", omm,
                         reads=[f"wpb{tag}{bi}"] +
                               [srckey_fn(fc, ch) for fc in range(DC) for ch in range(2)],
                         writes=["pso0", "pso1"])
                    for ch in range(2):
                        p.op("dv", (lambda e, j=j, ch=ch: e.tensor_tensor(
                            out=xs[:, j, ch * 512:(ch + 1) * 512],
                            in0=pso[:, ch, :],
                            in1=xs[:, j, ch * 512:(ch + 1) * 512], op=ALU.add)),
                            reads=[f"pso{ch}", f"x{j}c{ch}"], writes=[f"x{j}c{ch}"])
            p.barrier()

        def emit_ffn(tag, make_scratch=False):
            for ch in range(2):
                mk = make_scratch and ch == 0
                sf = ExitStack()
                lnf = sf.enter_context(nc.sbuf_tensor(f"lnf{tag}{ch}", [128, DC, 512], BF16))
                emit_ln(f"{tag}c{ch}",
                        xs[:, :, ch * 512:(ch + 1) * 512], xkeys(ch),
                        lnf[:], [f"lnf_{dc}" for dc in range(DC)])
                gated = sf.enter_context(nc.sbuf_tensor(f"gated{tag}{ch}",
                                                        [128, HC, 512], BF16))
                with nc.sbuf_tensor(f"wgb{tag}{ch}", [128, 2, 2, DC, 128], BF16) as wgb, \
                     nc.sbuf_tensor(f"wub{tag}{ch}", [128, 2, 2, DC, 128], BF16) as wub, \
                     nc.sbuf_tensor(f"sgb{tag}{ch}", [128, 2, 2, 512], F32) as sgb, \
                     nc.psum_tensor(f"psg{tag}{ch}", [128, 2, 2, 512], F32) as psg, \
                     nc.psum_tensor(f"psu{tag}{ch}", [128, 2, 2, 512], F32) as psu:
                    def ldgu(g):
                        bi = g % 2
                        if mk:
                            dma("pl", wgb[:, bi, :, :, :],
                                dWg[2 * g:2 * g + 2]
                                .rearrange("h p (dc o) -> p h dc o", o=128),
                                writes=[f"wgb{bi}"])
                            dma("pl", wub[:, bi, :, :, :],
                                dWu[2 * g:2 * g + 2]
                                .rearrange("h p (dc o) -> p h dc o", o=128),
                                writes=[f"wub{bi}"])
                            dma("sy", dWgS[2 * g:2 * g + 2]
                                .rearrange("h p (dc o) -> p h dc o", o=128),
                                wgb[:, bi, :, :, :],
                                reads=[f"wgb{bi}"], writes=[f"wgscr{g}"])
                            dma("sy", dWuS[2 * g:2 * g + 2]
                                .rearrange("h p (dc o) -> p h dc o", o=128),
                                wub[:, bi, :, :, :],
                                reads=[f"wub{bi}"], writes=[f"wuscr{g}"])
                        else:
                            dma("sy", wgb[:, bi, :, :, :],
                                dWgS[2 * g:2 * g + 2]
                                .rearrange("h p (dc o) -> p h dc o", o=128),
                                reads=[f"wgscr{g}"], writes=[f"wgb{bi}"])
                            dma("sy", wub[:, bi, :, :, :],
                                dWuS[2 * g:2 * g + 2]
                                .rearrange("h p (dc o) -> p h dc o", o=128),
                                reads=[f"wuscr{g}"], writes=[f"wub{bi}"])
                    ldgu(0)
                    for g in range(HC // 2):
                        bi = g % 2
                        if g + 1 < HC // 2:
                            ldgu(g + 1)
                        def gumm(e, bi=bi):
                            r = None
                            for dc in range(DC):
                                for hh in range(2):
                                    e.matmul(out=psg[:, bi, hh, :],
                                             lhsT=wgb[:, bi, hh, dc, :],
                                             rhs=lnf[:, dc, :],
                                             start=(dc == 0), stop=(dc == DC - 1))
                                    r = e.matmul(out=psu[:, bi, hh, :],
                                                 lhsT=wub[:, bi, hh, dc, :],
                                                 rhs=lnf[:, dc, :],
                                                 start=(dc == 0), stop=(dc == DC - 1))
                            return r
                        p.op("pe", gumm,
                             reads=[f"wgb{bi}", f"wub{bi}"] +
                                   [f"lnf_{dc}" for dc in range(DC)],
                             writes=[f"psg{bi}", f"psu{bi}"])
                        p.op("ac", (lambda e, bi=bi: e.activation(
                            out=sgb[:, bi, :, :], in_=psg[:, bi, :, :], func=AF.Silu)),
                            reads=[f"psg{bi}"], writes=[f"sgb{bi}"])
                        p.op("dv", (lambda e, g2=g, bi=bi: e.tensor_tensor(
                            out=gated[:, 2 * g2:2 * g2 + 2, :], in0=sgb[:, bi, :, :],
                            in1=psu[:, bi, :, :], op=ALU.mult)),
                            reads=[f"sgb{bi}", f"psu{bi}"],
                            writes=[f"g{2 * g}", f"g{2 * g + 1}"])
                p.barrier()
                with nc.sbuf_tensor(f"wdb{tag}{ch}", [128, 2, HC, 128], BF16) as wdb, \
                     nc.psum_tensor(f"psd{tag}{ch}", [128, 2, 512], F32) as psd:
                    gkeys = [f"g{hc}" for hc in range(HC)]
                    for j in range(16):
                        bi = j % 2
                        for hhalf in range(2):
                            if mk:
                                dma("pl", wdb[:, bi, hhalf * 32:(hhalf + 1) * 32, :],
                                    dWd[j][:, hhalf * 4096:(hhalf + 1) * 4096]
                                    .rearrange("p (hc o) -> p hc o", o=128),
                                    writes=[f"wdb{bi}h{hhalf}"])
                                dma("sy", dWdS[j][:, hhalf * 4096:(hhalf + 1) * 4096]
                                    .rearrange("p (hc o) -> p hc o", o=128),
                                    wdb[:, bi, hhalf * 32:(hhalf + 1) * 32, :],
                                    reads=[f"wdb{bi}h{hhalf}"],
                                    writes=[f"wdscr{j}h{hhalf}"])
                            else:
                                dma("sy", wdb[:, bi, hhalf * 32:(hhalf + 1) * 32, :],
                                    dWdS[j][:, hhalf * 4096:(hhalf + 1) * 4096]
                                    .rearrange("p (hc o) -> p hc o", o=128),
                                    reads=[f"wdscr{j}h{hhalf}"],
                                    writes=[f"wdb{bi}h{hhalf}"])
                        def dmm(e, bi=bi):
                            r = None
                            for hc in range(HC):
                                r = e.matmul(out=psd[:, bi, :], lhsT=wdb[:, bi, hc, :],
                                             rhs=gated[:, hc, :],
                                             start=(hc == 0), stop=(hc == HC - 1))
                            return r
                        p.op("pe", dmm,
                             reads=[f"wdb{bi}h0", f"wdb{bi}h1"] + gkeys,
                             writes=[f"psd{bi}"])
                        p.op("dv", (lambda e, j=j, ch=ch, bi=bi: e.tensor_tensor(
                            out=xs[:, j, ch * 512:(ch + 1) * 512], in0=psd[:, bi, :],
                            in1=xs[:, j, ch * 512:(ch + 1) * 512], op=ALU.add)),
                            reads=[f"psd{bi}", f"x{j}c{ch}"], writes=[f"x{j}c{ch}"])
                sf.close()
                p.barrier()

        def emit_attn_b():
            s3 = ExitStack()
            qbfm = s3.enter_context(nc.sbuf_tensor("qbfm", [128, DC, NT], BF16))
            with nc.sbuf_tensor("yb3", [128, DC, NT], BF16) as yb:
                for ch in range(2):
                    emit_ln("s3c%d" % ch,
                            xs[:, :, ch * 512:(ch + 1) * 512], xkeys(ch),
                            yb[:, :, ch * 512:(ch + 1) * 512],
                            [f"y3_{dc}c{ch}" for dc in range(DC)], mode="rms")
                with nc.sbuf_tensor("wqbb", [128, 2, DC, 128], BF16) as wqbb, \
                     nc.psum_tensor("psq2", [128, 2, 512], F32) as psq2:
                    for j in range(16):
                        bi = j % 2
                        dma("pl", wqbb[:, bi, :, :],
                            dWqb[j].rearrange("p (dc o) -> p dc o", o=128),
                            writes=[f"wqbb{bi}"])
                        for cc in range(2):
                            def q2mm(e, bi=bi, cc=cc):
                                r = None
                                for dc in range(DC):
                                    r = e.matmul(out=psq2[:, cc, :], lhsT=wqbb[:, bi, dc, :],
                                                 rhs=yb[:, dc, cc * 512:(cc + 1) * 512],
                                                 start=(dc == 0), stop=(dc == DC - 1))
                                return r
                            p.op("pe", q2mm,
                                 reads=[f"wqbb{bi}"] + [f"y3_{dc}c{cc}" for dc in range(DC)],
                                 writes=[f"psq2{cc}"])
                            p.op("dv", (lambda e, j=j, cc=cc: e.tensor_copy(
                                out=qbfm[:, j, cc * 512:(cc + 1) * 512], in_=psq2[:, cc, :])),
                                reads=[f"psq2{cc}"], writes=[f"qb{j}c{cc}"])
            p.barrier()
            ob_fm = s3.enter_context(nc.sbuf_tensor("ob_fm", [128, DC, NT], BF16))
            s3b = ExitStack()
            pmkb = s3b.enter_context(nc.sbuf_tensor("pmkb", [128, 16, 32], BF16))
            pmvb = s3b.enter_context(nc.sbuf_tensor("pmvb", [32, 16, 128], BF16))
            bmaskb = s3b.enter_context(nc.sbuf_tensor("bmaskb", [128, 512], F32))
            smf = s3b.enter_context(nc.sbuf_tensor("smf", [128, 2, 512], F32))
            ebf = s3b.enter_context(nc.sbuf_tensor("ebf", [128, 2, 512], F32))
            smx = s3b.enter_context(nc.sbuf_tensor("smx", [128, 2, 32], F32))
            sms = s3b.enter_context(nc.sbuf_tensor("sms", [128, 2, 32], F32))
            pb2 = s3b.enter_context(nc.sbuf_tensor("pb2", [128, 2, 512], BF16))
            ptb = s3b.enter_context(nc.sbuf_tensor("ptb", [32, 2, 16, 128], BF16))
            dma("pl", pmkb[:], dPmk, writes=["pmkb"])
            dma("pl", pmvb[:], dPmv, writes=["pmvb"])
            dma("sy", bmaskb[:], dBmask, writes=["bmaskb"])
            with nc.psum_tensor("psS", [128, 2, 512], F32) as psS, \
                 nc.psum_tensor("psT2", [32, 16, 128], BF16) as psT2, \
                 nc.psum_tensor("psV2", [128, 2048], F32) as psV2:

                def b_smm(t):
                    a2 = t % 2
                    tc = t * 128

                    def bsmm(e, a2=a2, tc=tc):
                        r = None
                        for j in range(16):
                            r = e.matmul(out=psS[:, a2, j * 32:(j + 1) * 32],
                                         lhsT=qbfm[:, j, tc:tc + 128],
                                         rhs=pmkb[:, j, :], start=True, stop=True)
                        return r
                    p.op("pe", bsmm,
                         reads=["pmkb"] + [f"qb{j}c{t // 4}" for j in range(16)],
                         writes=[f"psS{a2}"])

                def b_chain(t):
                    a2 = t % 2
                    smfv = smf[:, a2, :]
                    smfk = f"smf{a2}"
                    ebfv = ebf[:, a2, :]
                    ebfk = f"ebf{a2}"
                    smxv = smx[:, a2, :]
                    smxk = f"smx{a2}"
                    smsv = sms[:, a2, :]
                    smsk = f"sms{a2}"
                    pb2v = pb2[:, a2, :]
                    pb2k = f"pb2{a2}"
                    p.op("dv", (lambda e, a2=a2, smfv=smfv: e.tensor_tensor(
                        out=smfv, in0=psS[:, a2, :], in1=bmaskb[:], op=ALU.add)),
                        reads=[f"psS{a2}", "bmaskb"], writes=[smfk])
                    smf3 = smfv.rearrange("p (h k) -> p h k", k=16)
                    p.op("dv", (lambda e, smf3=smf3, smxv=smxv: e.tensor_reduce(
                        out=smxv, in_=smf3, axis=AX.X, op=ALU.max)),
                        reads=[smfk], writes=[smxk])
                    p.op("dv", (lambda e, smf3=smf3, smxv=smxv: e.tensor_tensor(
                        out=smf3, in0=smf3,
                        in1=smxv.unsqueeze(-1).broadcast_to((128, 32, 16)),
                        op=ALU.subtract)),
                        reads=[smfk, smxk], writes=[smfk])
                    p.op("ac", (lambda e, smfv=smfv, ebfv=ebfv: e.activation(
                        out=ebfv, in_=smfv, func=AF.Exp, scale=0.125)),
                        reads=[smfk], writes=[ebfk])
                    ebf3 = ebfv.rearrange("p (h k) -> p h k", k=16)
                    p.op("dv", (lambda e, ebf3=ebf3, smsv=smsv: e.tensor_reduce(
                        out=smsv, in_=ebf3, axis=AX.X, op=ALU.add)),
                        reads=[ebfk], writes=[smsk])
                    p.op("ac", (lambda e, smsv=smsv, smxv=smxv: e.activation(
                        out=smxv, in_=smsv, func=AF.Ln)),
                        reads=[smsk], writes=[smxk + "l"])
                    p.op("ac", (lambda e, smxv=smxv: e.activation(
                        out=smxv, in_=smxv, func=AF.Exp, scale=-1.0)),
                        reads=[smxk + "l"], writes=[smxk])
                    pb23 = pb2v.rearrange("p (h k) -> p h k", k=16)
                    p.op("dv", (lambda e, ebf3=ebf3, pb23=pb23, smxv=smxv:
                                e.tensor_tensor(
                        out=pb23, in0=ebf3,
                        in1=smxv.unsqueeze(-1).broadcast_to((128, 32, 16)),
                        op=ALU.mult)),
                        reads=[ebfk, smxk], writes=[pb2k])

                def b_tail(t):
                    a2 = t % 2
                    tc = t * 128
                    pb2v = pb2[:, a2, :]
                    pb2k = f"pb2{a2}"
                    ptbv = ptb[:, a2, :, :]
                    ptbk = f"ptb{a2}"

                    def btr(e, pb2v=pb2v, ptbv=ptbv):
                        r = None
                        for j in range(16):
                            r = e.transpose(out=psT2[:, j, :],
                                            in_=pb2v[:, j * 32:(j + 1) * 32],
                                            identity=ident[:])
                        return r
                    p.op("pe", btr, reads=[pb2k, "ident"], writes=["psT2"])
                    p.op("dv", (lambda e, ptbv=ptbv: e.tensor_copy(
                        out=ptbv, in_=psT2[:])),
                        reads=["psT2"], writes=[ptbk])

                    def bpv(e, ptbv=ptbv):
                        r = None
                        for j in range(16):
                            r = e.matmul(out=psV2[:, j * 128:(j + 1) * 128],
                                         lhsT=pmvb[:, j, :], rhs=ptbv[:, j, :],
                                         start=True, stop=True)
                        return r
                    p.op("pe", bpv, reads=[ptbk, "pmvb"], writes=["psV2"])
                    p.op("dv", (lambda e, tc=tc: e.tensor_copy(
                        out=ob_fm[:, :, tc:tc + 128],
                        in_=psV2[:].rearrange("p (j q) -> p j q", q=128))),
                        reads=["psV2"],
                        writes=[f"ob{j}c{t // 4}" for j in range(16)])

                for t in range(8):
                    b_smm(t)
                    if t >= 1:
                        b_tail(t - 1)
                    b_chain(t)
                b_tail(7)
            p.barrier()
            s3b.close()
            p.barrier()
            emit_proj_residual("wo3", dWout, ob_fm, (lambda fc, ch: f"ob{fc}c{ch}"))
            s3.close()
            p.barrier()

        if 1 in phases:
            s1 = ExitStack()
            qrot = s1.enter_context(nc.sbuf_tensor("qrot", [128, NPAIR, 2, NT], BF16))
            kbuf = s1.enter_context(nc.sbuf_tensor("kbuf", [128, 2, 2, HALO + NT], BF16))
            vbuf = s1.enter_context(nc.sbuf_tensor("vbuf", [128, NSLOT, 512], BF16))
            ln1 = s1.enter_context(nc.sbuf_tensor("ln1", [128, DC, NT], BF16))
            o_fm = ln1
            s1mid = ExitStack()
            lnh = s1mid.enter_context(nc.sbuf_tensor("lnh", [128, DC, HALO], BF16))
            for ch in range(2):
                emit_ln("s1o%d" % ch,
                        xs[:, :, ch * 512:(ch + 1) * 512], xkeys(ch),
                        ln1[:, :, ch * 512:(ch + 1) * 512],
                        [f"ln1_{dc}c{ch}" for dc in range(DC)])
            xs_flat = xs[:].rearrange("p a b -> p (a b)")

            def carve(off, shape, dtype):
                n = 1
                for d in shape[1:]:
                    n *= d
                if dtype == F32:
                    v = xs_flat[:, off:off + n]
                else:
                    v = xs_flat[:, off:off + (n + 1) // 2].bitcast(BF16)
                if len(shape) == 2:
                    return v
                pat = "p (" + " ".join(chr(97 + k) for k in range(len(shape) - 1)) + ") -> p " + \
                      " ".join(chr(97 + k) for k in range(len(shape) - 1))
                kw = {chr(97 + k): shape[k + 1] for k in range(len(shape) - 1)}
                return v.rearrange(pat, **kw)

            for h in range(2):
                dma("sy", dXspill[:, :, h * 512:(h + 1) * 512],
                    xs[:, :, h * 512:(h + 1) * 512],
                    reads=xkeys(h), writes=[f"xsp{h}"])
            p.barrier()
            xh = carve(0, [128, DC, HALO], F32)
            dma("sy", xh, dXh, writes=[f"xh{dc}" for dc in range(DC)])
            emit_ln("s1h", xh, [f"xh{dc}" for dc in range(DC)],
                    lnh[:], [f"lnh{dc}" for dc in range(DC)], W=512)
            p.barrier()

            tap("dbg_ln1", ln1[:], [128, DC, NT], BF16,
                [f"ln1_{dc}c{ch}" for dc in range(DC) for ch in range(2)])
            tap("dbg_lnh", lnh[:], [128, DC, HALO], BF16,
                [f"lnh{dc}h{hh}" for dc in range(DC) for hh in range(4)])
            s1r = ExitStack()
            cosb = s1r.enter_context(nc.sbuf_tensor("cosb_s", [128, HALO + NT], BF16))
            sinb = s1r.enter_context(nc.sbuf_tensor("sinb_s", [128, HALO + NT], BF16))
            dma("sy", cosb[:], dCos, writes=["cos"])
            dma("sy", sinb[:], dSin, writes=["sin"])
            rc0 = carve(0, [128, 2, 512], F32)
            rtA = carve(1024, [128, 2, 512], F32)

            def emit_rope(psq, kps, tabcol, out0, out1, k0, k1, ch=0):
                cs = cosb[:, tabcol:tabcol + 512]
                sn = sinb[:, tabcol:tabcol + 512]
                rc = rc0[:, ch, :]
                rt = rtA[:, ch, :]
                kr = kps + "rc%d" % ch
                kt = kps + "rt%d" % ch
                ks = kps + "s0%d" % ch
                p.op("dv", (lambda e, psq=psq, rc=rc: e.tensor_copy(
                    out=rc, in_=psq[:, 0:512])),
                    reads=[kps], writes=[kr])
                p.op("dv", (lambda e, psq=psq, sn=sn, rt=rt: e.tensor_tensor(
                    out=rt, in0=psq[:, 512:1024], in1=sn, op=ALU.mult)),
                    reads=[kps, "sin"], writes=[kt])
                p.op("dv", (lambda e, psq=psq, cs=cs, rc=rc: e.tensor_tensor(
                    out=psq[:, 0:512], in0=rc, in1=cs, op=ALU.mult)),
                    reads=[kr, "cos", kps], writes=[ks])
                p.op("dv", (lambda e, psq=psq, out0=out0, rt=rt: e.tensor_tensor(
                    out=out0, in0=psq[:, 0:512], in1=rt, op=ALU.subtract)),
                    reads=[ks, kt], writes=[k0])
                p.op("dv", (lambda e, psq=psq, cs=cs, rt=rt: e.tensor_tensor(
                    out=rt, in0=psq[:, 512:1024], in1=cs, op=ALU.mult)),
                    reads=[kps, "cos", kt], writes=[kt])
                p.op("dv", (lambda e, psq=psq, sn=sn, rc=rc: e.tensor_tensor(
                    out=psq[:, 0:512], in0=rc, in1=sn, op=ALU.mult)),
                    reads=[kr, "sin", ks], writes=[ks])
                p.op("dv", (lambda e, psq=psq, out1=out1, rt=rt: e.tensor_tensor(
                    out=out1, in0=rt, in1=psq[:, 0:512], op=ALU.add)),
                    reads=[kt, ks], writes=[k1])

            with nc.sbuf_tensor("wqb", [128, 2, 2, DC, 128], BF16) as wqb, \
                 nc.psum_tensor("psq", [128, 2, 2, 2, 512], F32) as psq:
                for i in range(NPAIR):
                    bi = i % 2
                    for s in range(2):
                        dma("pl", wqb[:, bi, s, :, :],
                            dWq[2 * i + s].rearrange("p (dc o) -> p dc o", o=128),
                            writes=[f"wqb{bi}{s}"])
                    def qmm(e, bi=bi):
                        r = None
                        for s in range(2):
                            for dc in range(DC):
                                for ch in range(2):
                                    r = e.matmul(out=psq[:, bi, ch, s, :],
                                                 lhsT=wqb[:, bi, s, dc, :],
                                                 rhs=ln1[:, dc, ch * 512:(ch + 1) * 512],
                                                 start=(dc == 0), stop=(dc == DC - 1))
                        return r
                    p.op("pe", qmm,
                         reads=[f"wqb{bi}0", f"wqb{bi}1"] +
                               [f"ln1_{dc}c{ch}" for dc in range(DC) for ch in range(2)],
                         writes=[f"psq{bi}", f"psq{bi}s00", f"psq{bi}s01"])
                    for ch in range(2):
                        emit_rope(psq[:, bi, ch, :, :].rearrange("p a b -> p (a b)"),
                                  f"psq{bi}", HALO + ch * 512,
                                  qrot[:, i, 0, ch * 512:(ch + 1) * 512],
                                  qrot[:, i, 1, ch * 512:(ch + 1) * 512],
                                  f"q{i}_0c{ch}", f"q{i}_1c{ch}", ch=ch)
            p.barrier()

            tap("dbg_qrot", qrot[:], [128, NPAIR, 2, NT], BF16,
                [f"q{i}_{s}c{ch}" for i in range(NPAIR) for s in range(2) for ch in range(2)])
            lnsrc = [((lambda dc: lnh[:, dc, :]), [f"lnh{dc}" for dc in range(DC)]),
                     ((lambda dc: ln1[:, dc, 0:512]), [f"ln1_{dc}c0" for dc in range(DC)]),
                     ((lambda dc: ln1[:, dc, 512:1024]),
                      [f"ln1_{dc}c1" for dc in range(DC)])]
            with nc.sbuf_tensor("wkb", [128, 2, DC, 128], BF16) as wkb, \
                 nc.psum_tensor("psk", [128, 3, 2, 512], F32) as psk:
                allk = [k for _, kk in lnsrc for k in kk]
                for g in range(2):
                    for s in range(2):
                        dma("pl", wkb[:, s, :, :],
                            dWk[2 * g + s].rearrange("p (dc o) -> p dc o", o=128),
                            writes=[f"wkb{s}"])
                    def kmm(e):
                        r = None
                        for s in range(2):
                            for dc in range(DC):
                                for c3 in range(3):
                                    r = e.matmul(out=psk[:, c3, s, :],
                                                 lhsT=wkb[:, s, dc, :],
                                                 rhs=lnsrc[c3][0](dc),
                                                 start=(dc == 0), stop=(dc == DC - 1))
                        return r
                    p.op("pe", kmm, reads=["wkb0", "wkb1"] + allk,
                         writes=["psk", "psks00", "psks01", "psks02"])
                    for c3 in range(3):
                        emit_rope(psk[:, c3, :, :].rearrange("p a b -> p (a b)"),
                                  "psk", c3 * 512,
                                  kbuf[:, g, 0, c3 * 512:(c3 + 1) * 512],
                                  kbuf[:, g, 1, c3 * 512:(c3 + 1) * 512],
                                  f"kb{g}_0c{c3}", f"kb{g}_1c{c3}", ch=c3 % 2)
            s1r.close()
            p.barrier()
            tap("dbg_kbuf", kbuf[:], [128, 2, 2, HALO + NT], BF16,
                [f"kb{g}_{s}c{c3}" for g in range(2) for s in range(2) for c3 in range(3)])

            with nc.sbuf_tensor("wvb", [128, DC, 512], BF16) as wvb, \
                 nc.psum_tensor("psv", [128, 512], F32) as psv:
                dma("pl", wvb[:], dWv.rearrange("(dc p) n -> p dc n", p=128), writes=["wvb"])
                for t in range(NSLOT):
                    if t < 4:
                        lhsf = (lambda dc, t=t: lnh[:, dc, t * 128:(t + 1) * 128])
                        lkeys = [f"lnh{dc}" for dc in range(DC)]
                    else:
                        lhsf = (lambda dc, t=t: ln1[:, dc, (t - 4) * 128:(t - 3) * 128])
                        lkeys = [f"ln1_{dc}c{(t - 4) // 4}" for dc in range(DC)]
                    def vmm(e, lhsf=lhsf):
                        r = None
                        for dc in range(DC):
                            r = e.matmul(out=psv[:], lhsT=lhsf(dc), rhs=wvb[:, dc, :],
                                         start=(dc == 0), stop=(dc == DC - 1))
                        return r
                    p.op("pe", vmm, reads=lkeys + ["wvb"], writes=["psv"])
                    p.op("dv", (lambda e, t=t: e.tensor_copy(out=vbuf[:, t, :], in_=psv[:])),
                         reads=["psv"], writes=[f"vb{t}"])
            s1mid.close()
            p.barrier()
            tap("dbg_vbuf", vbuf[:], [128, NSLOT, 512], BF16,
                [f"vb{t}" for t in range(NSLOT)])

            s1b = ExitStack()
            amask = s1b.enter_context(nc.sbuf_tensor("amask_s", [128, 8, 640], BF16))
            dma("sy", amask[:], dAmask, writes=["amask"])
            amb_v = carve(0, [128, 4, 2, 640], F32)
            eb_v = carve(5120, [128, 4, 2, 640], F32)
            tb_v = carve(10240, [128, 2, 640], F32)
            pbuf_v = carve(11520, [128, 4, 640], BF16)
            pts_v = carve(12800, [128, 4, 640], BF16)
            sts_v = carve(14080, [128, 4, 14], F32)

            with nc.psum_tensor("psA_", [128, 1280], F32) as psA_, \
                 nc.psum_tensor("psB_", [128, 1280], F32) as psB_, \
                 nc.psum_tensor("psT", [128, 640], BF16) as psT, \
                 nc.psum_tensor("psO", [128, 256], F32) as psO:
                units = []
                for i in range(NPAIR):
                    g = i // 4
                    hA = i if i < 4 else i + 4
                    for qi in range(8):
                        for half in range(2):
                            units.append((i, g, hA, qi, half))

                def emit_smm(i, g, qi):
                    qc = qi * 128
                    qkeys = [f"q{i}_{s}c{qi // 4}" for s in range(2)]
                    kkeys = sorted({f"kb{g}_{s}c{(qi + rr) // 4}"
                                    for s in range(2) for rr in range(5)})

                    def smm(e, i=i, g=g, qi=qi, qc=qc):
                        r = None
                        for s in range(2):
                            for rr in range(5):
                                slot = qi + rr
                                o0 = s * 640 + rr * 128
                                for half, ps in ((0, psA_), (1, psB_)):
                                    lo = 0 if half == 0 else 64
                                    tp = None if half == 0 else (64, 0)
                                    r = e.matmul(
                                        out=ps[:, o0:o0 + 128],
                                        lhsT=qrot[lo:lo + 64, i, s, qc:qc + 128],
                                        rhs=kbuf[lo:lo + 64, g, s,
                                                 slot * 128:slot * 128 + 128],
                                        start=True, stop=True, tile_position=tp)
                        return r
                    p.op("pe", smm, reads=qkeys + kkeys, writes=["psA", "psB"])

                def s1_dv(k):
                    i, g, hA, qi, half = units[k]
                    cu = k % 4
                    sv = sts_v[:, cu, :]
                    sk = f"sts{cu}_"
                    kp = "psA" if half == 0 else "psB"
                    psH = psA_ if half == 0 else psB_
                    am2v = amb_v[:, cu, :, :]
                    amk = f"am{cu}"
                    p.op("dv", (lambda e, qi=qi, am2v=am2v, psH=psH: e.tensor_tensor(
                        out=am2v, in0=psH[:].rearrange("p (a k) -> p a k", k=640),
                        in1=amask[:, qi, :].unsqueeze(1).broadcast_to((128, 2, 640)),
                        op=ALU.add)),
                        reads=[kp, "amask"], writes=[amk])
                    p.op("dv", (lambda e, sv=sv, am2v=am2v: e.tensor_reduce(
                        out=sv[:, 0:2], in_=am2v, axis=AX.X, op=ALU.max)),
                        reads=[amk], writes=[sk + "m"])
                    p.op("dv", (lambda e, sv=sv: e.tensor_scalar(
                        out=sv[:, 2:4], in0=sv[:, 0:2], scalar1=-0.125,
                        scalar2=None, op0=ALU.mult)),
                        reads=[sk + "m"], writes=[sk + "b"])

                def s2_ac(k):
                    i, g, hA, qi, half = units[k]
                    cu = k % 4
                    sv = sts_v[:, cu, :]
                    sk = f"sts{cu}_"
                    am2v = amb_v[:, cu, :, :]
                    amk = f"am{cu}"
                    ebv = eb_v[:, cu, :, :]
                    ebk = f"eb{cu}"
                    p.op("ac", (lambda e, sv=sv, am2v=am2v, ebv=ebv: e.activation(
                        out=ebv[:, 0, :], in_=am2v[:, 0, :], func=AF.Exp,
                        bias=sv[:, 2:3], scale=0.125, accum_out=sv[:, 4:5])),
                        reads=[amk, sk + "b"], writes=[ebk + "1", sk + "s1"])
                    p.op("ac", (lambda e, sv=sv, am2v=am2v, ebv=ebv: e.activation(
                        out=ebv[:, 1, :], in_=am2v[:, 1, :], func=AF.Exp,
                        bias=sv[:, 3:4], scale=0.125, accum_out=sv[:, 5:6])),
                        reads=[amk, sk + "b"], writes=[ebk + "2", sk + "s2"])
                    p.op("ac", (lambda e, sv=sv: e.copy(
                        out=sv[:, 12:14], in_=sv[:, 4:6])),
                        reads=[sk + "s1", sk + "s2"],
                        writes=[sk + "s1f", sk + "s2f"])
                    p.op("ac", (lambda e, sv=sv: e.activation(
                        out=sv[:, 6:7], in_=sv[:, 13:14], func=AF.Ln)),
                        reads=[sk + "s2f"], writes=[sk + "lr2"])
                    p.op("ac", (lambda e, sv=sv: e.activation(
                        out=sv[:, 6:7], in_=sv[:, 6:7], func=AF.Exp, scale=-1.0)),
                        reads=[sk + "lr2"], writes=[sk + "r2"])

                def s3_dv(k):
                    i, g, hA, qi, half = units[k]
                    cu = k % 4
                    c2 = k % 2
                    sv = sts_v[:, cu, :]
                    sk = f"sts{cu}_"
                    ebv = eb_v[:, cu, :, :]
                    ebk = f"eb{cu}"
                    tbv = tb_v[:, c2, :]
                    tbk = f"tb{c2}"
                    h = hA + 4 * half
                    p.op("dv", (lambda e, sv=sv: e.tensor_tensor(
                        out=sv[:, 7:8], in0=sv[:, 12:13], in1=sv[:, 6:7],
                        op=ALU.mult)),
                        reads=[sk + "s1f", sk + "r2"], writes=[sk + "c"])
                    p.op("dv", (lambda e, sv=sv, h=h: e.tensor_scalar(
                        out=sv[:, 8:9], in0=sv[:, 7:8],
                        scalar1=lamn[:, h:h + 1], scalar2=None, op0=ALU.mult)),
                        reads=[sk + "c", "lamn"], writes=[sk + "cn"])
                    p.op("dv", (lambda e, sv=sv, ebv=ebv, tbv=tbv:
                                e.scalar_tensor_tensor(
                        out=tbv, in0=ebv[:, 1, :], scalar=sv[:, 8:9],
                        in1=ebv[:, 0, :], op0=ALU.mult, op1=ALU.add)),
                        reads=[ebk + "1", ebk + "2", sk + "cn"],
                        sem_reads=[sk + "s1f"], writes=[tbk])
                    p.op("dv", (lambda e, sv=sv, tbv=tbv, cu=cu: e.tensor_scalar(
                        out=amb_v[:, cu, 0, :], in0=tbv, scalar1=0.0, scalar2=0.0,
                        op0=ALU.max, op1=ALU.add, accum_out=sv[:, 9:10])),
                        reads=[tbk], writes=[f"am{cu}", sk + "S"])
                    p.op("dv", (lambda e, sv=sv: e.scalar_tensor_tensor(
                        out=sv[:, 10:11], in0=sv[:, 12:13], scalar=1e-6,
                        in1=sv[:, 9:10], op0=ALU.mult, op1=ALU.add)),
                        reads=[sk + "s1f"], sem_reads=[sk + "S"], writes=[sk + "dn"])

                def s4_ac(k):
                    i, g, hA, qi, half = units[k]
                    cu = k % 4
                    c2 = k % 2
                    sv = sts_v[:, cu, :]
                    sk = f"sts{cu}_"
                    tbv = tb_v[:, c2, :]
                    tbk = f"tb{c2}"
                    pbv = pbuf_v[:, cu, :]
                    p.op("ac", (lambda e, sv=sv: e.activation(
                        out=sv[:, 11:12], in_=sv[:, 10:11], func=AF.Ln)),
                        reads=[sk + "dn"], writes=[sk + "lr"])
                    p.op("ac", (lambda e, sv=sv: e.activation(
                        out=sv[:, 11:12], in_=sv[:, 11:12], func=AF.Exp,
                        scale=-1.0)),
                        reads=[sk + "lr"], writes=[sk + "r"])
                    p.op("ac", (lambda e, sv=sv, tbv=tbv, pbv=pbv: e.activation(
                        out=pbv, in_=tbv, func=AF.Relu, scale=sv[:, 11:12])),
                        reads=[tbk], sem_reads=[sk + "r"], writes=[f"pb{cu}"])

                def s5_tail(k):
                    i, g, hA, qi, half = units[k]
                    cu = k % 4
                    qc = qi * 128

                    def trg(e, cu=cu):
                        r = None
                        for rr in range(5):
                            r = e.transpose(out=psT[:, rr * 128:(rr + 1) * 128],
                                            in_=pbuf_v[:, cu, rr * 128:(rr + 1) * 128],
                                            identity=ident[:])
                        return r
                    p.op("pe", trg, reads=[f"pb{cu}", "ident"], writes=["psT"])
                    p.op("dv", (lambda e, cu=cu: e.tensor_copy(
                        out=pts_v[:, cu, :], in_=psT[:])),
                        reads=["psT"], writes=[f"pts{cu}"])

                    def pvg(e, g=g, qi=qi, half=half, cu=cu):
                        r = None
                        kvh = 2 * g + half
                        for rr in range(5):
                            slot = qi + rr
                            r = e.matmul(
                                out=psO[:, half * 128:half * 128 + 128],
                                lhsT=vbuf[:, slot, kvh * 128:kvh * 128 + 128],
                                rhs=pts_v[:, cu, rr * 128:(rr + 1) * 128],
                                start=(rr == 0), stop=(rr == 4))
                        return r
                    p.op("pe", pvg,
                         reads=[f"pts{cu}"] + [f"vb{qi + rr}" for rr in range(5)],
                         writes=[f"psO{half}"])
                    if half == 1:
                        p.op("dv", (lambda e, hA=hA, qc=qc: e.tensor_copy(
                            out=o_fm[:, hA:hA + 5:4, qc:qc + 128],
                            in_=psO[:].rearrange("p (h q) -> p h q", q=128))),
                            reads=["psO0", "psO1"],
                            writes=[f"of{hA}c{qi // 4}", f"of{hA + 4}c{qi // 4}"])

                NU = len(units)
                for k in range(NU + 4):
                    if k < NU:
                        i, g, hA, qi, half = units[k]
                        if half == 0:
                            emit_smm(i, g, qi)
                        s1_dv(k)
                    if 0 <= k - 1 < NU:
                        s2_ac(k - 1)
                    if 0 <= k - 2 < NU:
                        s3_dv(k - 2)
                    if 0 <= k - 3 < NU:
                        s4_ac(k - 3)
                    if 0 <= k - 4 < NU:
                        s5_tail(k - 4)
            p.barrier()
            s1b.close()
            p.barrier()
            for h in range(2):
                dma("sy", xs[:, :, h * 512:(h + 1) * 512],
                    dXspill[:, :, h * 512:(h + 1) * 512],
                    reads=[f"xsp{h}"], writes=xkeys(h))
            emit_proj_residual("wo1", dWo, o_fm, (lambda fc, ch: f"of{fc}c{ch}"))
            s1.close()
            p.barrier()

        if 2 in phases:
            emit_ffn("f2", make_scratch=(4 in phases or True))
        if 3 in phases:
            emit_attn_b()
        if 4 in phases:
            emit_ffn("f4")

        for h in range(2):
            dma("sy", dY[:, :, h * 512:(h + 1) * 512], xs[:, :, h * 512:(h + 1) * 512],
                reads=xkeys(h), writes=[f"yst{h}"])
        p.op("sy", lambda e: e.nop(), reads=["yst0", "yst1"])

        blk = st.enter_context(nc.Block())
        p.finalize_and_emit(nc, blk, st)
    return nc


def _tile_cols(W, colmaps):
    dcs = W.shape[0] // 128
    out = np.empty((len(colmaps), 128, W.shape[0]), dtype=np.float32)
    for j, cols in enumerate(colmaps):
        t = W[:, cols].reshape(dcs, 128, len(cols)).transpose(1, 0, 2)
        out[j] = t.reshape(128, dcs * len(cols))
    return out


def _tile_rowsT(W):
    nh = W.shape[0] // 128
    nd = W.shape[1] // 128
    out = np.empty((nh, 128, W.shape[1]), np.float32)
    for h in range(nh):
        blockT = W[h * 128:(h + 1) * 128, :].T
        out[h] = blockT.reshape(nd, 128, 128).transpose(1, 0, 2).reshape(128, W.shape[1])
    return out


def _fmify(rows):
    T = rows.shape[0]
    return np.ascontiguousarray(rows.reshape(T, DC, 128).transpose(2, 1, 0))


def kernel(L0_attn_a_x, L0_attn_a_Wq, L0_attn_a_Wk, L0_attn_a_Wv, L0_attn_a_Wo,
           L0_attn_a_lam, L0_attn_b_w_norm, L0_attn_b_w_qkv, L0_attn_b_w_out,
           L0_attn_b_pm, L0_ffn_w_gate, L0_ffn_w_up, L0_ffn_w_down):
    X = np.asarray(L0_attn_a_x, dtype=np.float32)
    Wq = np.asarray(L0_attn_a_Wq, dtype=np.float32)
    Wk = np.asarray(L0_attn_a_Wk, dtype=np.float32)
    Wv = np.ascontiguousarray(np.asarray(L0_attn_a_Wv, dtype=np.float32))
    Wo = np.asarray(L0_attn_a_Wo, dtype=np.float32)
    lam = np.asarray(L0_attn_a_lam, dtype=np.float32)
    wnorm = np.asarray(L0_attn_b_w_norm, dtype=np.float32)
    wqkv = np.asarray(L0_attn_b_w_qkv, dtype=np.float32)
    woutb = np.asarray(L0_attn_b_w_out, dtype=np.float32)
    pm = np.asarray(L0_attn_b_pm, dtype=np.float32)
    wg = np.asarray(L0_ffn_w_gate, dtype=np.float32)
    wu = np.asarray(L0_ffn_w_up, dtype=np.float32)
    wd = np.asarray(L0_ffn_w_down, dtype=np.float32)

    ar = np.arange(64)
    wq_maps = []
    for i in range(8):
        hA = i if i < 4 else i + 4
        for s in range(2):
            wq_maps.append(np.concatenate([hA * 128 + s * 64 + ar,
                                           (hA + 4) * 128 + s * 64 + ar]))
    wq_t = _tile_cols(Wq, wq_maps)
    wk_maps = []
    for g in range(2):
        for s in range(2):
            wk_maps.append(np.concatenate([(2 * g) * 128 + s * 64 + ar,
                                           (2 * g + 1) * 128 + s * 64 + ar]))
    wk_t = _tile_cols(Wk, wk_maps)
    c128 = np.arange(128)
    wo_t = _tile_cols(Wo, [j * 128 + c128 for j in range(16)])
    wqb_t = _tile_cols(np.ascontiguousarray(wqkv[:, :2048]),
                       [j * 128 + c128 for j in range(16)])
    wout_t = _tile_cols(woutb, [j * 128 + c128 for j in range(16)])
    wg_t = _tile_rowsT(wg)
    wu_t = _tile_rowsT(wu)
    wd_t = _tile_rowsT(wd)

    pmk_bd = np.zeros((128, 16, 32), np.float32)
    pmv_bd = np.zeros((32, 16, 128), np.float32)
    for j in range(16):
        pmk_bd[0:64, j, 0:16] = pm[0][2 * j].T
        pmk_bd[64:128, j, 16:32] = pm[0][2 * j + 1].T
        pmv_bd[0:16, j, 0:64] = pm[1][2 * j]
        pmv_bd[16:32, j, 64:128] = pm[1][2 * j + 1]

    pcol = np.arange(128)
    bmask = np.where((np.arange(512)[None, :] % 16) <= (pcol[:, None] % 16),
                     0.0, NEG).astype(np.float32)
    lam_bc = np.ascontiguousarray(np.broadcast_to(lam[None, :], (128, 16))).astype(np.float32)
    wn_fm = np.ascontiguousarray(wnorm.reshape(DC, 128).T)
    identity = np.eye(128, dtype=ml_dtypes.bfloat16)

    inv_freq = 1.0 / (10000.0 ** (np.arange(0, 128, 2, dtype=np.float64) / 128.0))
    shared = dict(wq_t=wq_t, wk_t=wk_t, wv=Wv, wo_t=wo_t, lam_bc=lam_bc,
                  wqb_t=wqb_t, wout_t=wout_t, pmk_bd=pmk_bd, pmv_bd=pmv_bd,
                  bmask=bmask, wn_fm=wn_fm, wg_t=wg_t, wu_t=wu_t, wd_t=wd_t,
                  ident=identity)

    in_maps = []
    for c in range(8):
        b, half = c // 2, c % 2
        T0 = half * NT
        xo = _fmify(X[b, T0:T0 + NT])
        if half == 1:
            xh = _fmify(X[b, T0 - HALO:T0])
        else:
            xh = np.zeros((128, DC, HALO), np.float32)
        pos = np.clip(np.arange(T0 - HALO, T0 + NT), 0, None).astype(np.float64)
        ang = pos[None, :] * inv_freq[:, None]
        cosb = np.concatenate([np.cos(ang), np.cos(ang)], axis=0).astype(np.float32)
        sinb = np.concatenate([np.sin(ang), np.sin(ang)], axis=0).astype(np.float32)
        qi_ = np.arange(8)[:, None, None]
        p_ = pcol[None, :, None]
        cix = np.arange(640)[None, None, :]
        qrow = T0 + qi_ * 128 + p_
        kpos = T0 + (qi_ + cix // 128) * 128 + (cix % 128) - HALO
        ok = (kpos >= 0) & (kpos <= qrow) & (kpos >= qrow - 511)
        am = np.where(ok, 0.0, NEG).astype(np.float32)
        amask = np.ascontiguousarray(am.transpose(1, 0, 2)).astype(ml_dtypes.bfloat16)
        m = dict(shared)
        m.update(xo=xo, xh=xh, cosb=cosb.astype(ml_dtypes.bfloat16),
                 sinb=sinb.astype(ml_dtypes.bfloat16), amask=amask)
        in_maps.append(m)

    nc = build_program(PHASES)
    res = run_bass_kernel_spmd(nc, in_maps, core_ids=list(range(8)))
    global LAST_RES
    LAST_RES = res
    out = np.empty((4, 2048, D), np.float32)
    for c in range(8):
        b, half = c // 2, c % 2
        y = np.asarray(res.results[c]["y"])
        out[b, half * NT:(half + 1) * NT] = y.transpose(2, 1, 0).reshape(NT, D)
    return out


# version 17
# speedup vs baseline: 1.0306x; 1.0306x over previous
import numpy as np
import ml_dtypes
import concourse.bass as bass
import concourse.mybir as mybir
from concourse.bass_utils import run_bass_kernel_spmd
from contextlib import ExitStack

F32 = mybir.dt.float32
BF16 = mybir.dt.bfloat16
AF = mybir.ActivationFunctionType
ALU = mybir.AluOpType
AX = mybir.AxisListType

D = 2048
DC = 16
NT = 1024
HALO = 512
NPAIR = 8
NKV = 4
NSLOT = 12
HC = 64
LN_EPS = 1e-5
RMS_EPS = float(np.finfo(np.float32).eps)
NEG = -1e30

PHASES = (1, 2, 3, 4)
TAP_I, TAP_QI = 1, 0
SERIAL_SCORES = False
DEBUG_TAPS = False
LAST_RES = None
BLOCKNAME = {"pe": "tensor", "dv": "vector", "ac": "scalar", "sy": "sync", "pl": "gpsimd"}
DMA_RR = {"sy": 2, "pl": 4}


class Prog:

    def __init__(self):
        self.items = []
        self.bufs = {}
        self.bar = None
        self.bar_taken = set()

    def op(self, eng, emit, reads=(), writes=(), dma=False, sem_reads=(), nopinc=False):
        opid = len(self.items)
        deps = set()

        def want(dep_opid, force=False):
            if dep_opid is None:
                return
            d = self.items[dep_opid]
            if not force and d["eng"] == eng and not d["dma"] and not dma:
                return
            deps.add(dep_opid)

        for b in sem_reads:
            e = self.bufs.get(b)
            if e is not None:
                want(e["w"], force=True)
        for b in reads:
            e = self.bufs.get(b)
            if e is not None:
                want(e["w"])
        for b in writes:
            e = self.bufs.get(b)
            if e is not None:
                want(e["w"], force=True)
                for r in e["r"].values():
                    want(r, force=True)
        if self.bar is not None and eng not in self.bar_taken:
            for oid in self.bar.values():
                want(oid)
            self.bar_taken.add(eng)
        self.items.append(dict(opid=opid, eng=eng, emit=emit, deps=deps, dma=dma,
                               nopinc=nopinc))
        rtag = "dma%d" % opid if dma else eng
        for b in list(sem_reads):
            self.bufs.setdefault(b, {"w": None, "r": {}})["r"][rtag] = opid
        for b in reads:
            self.bufs.setdefault(b, {"w": None, "r": {}})["r"][rtag] = opid
        for b in writes:
            self.bufs[b] = {"w": opid, "r": {}}
        return opid

    def barrier(self):
        last = {}
        for it in self.items:
            last[it["eng"]] = it["opid"]
        self.bar = last
        self.bar_taken = set()

    def finalize_and_emit(self, nc, blk, st):
        rr_cnt = {e: 0 for e in DMA_RR}
        for it in self.items:
            if it["dma"]:
                it["sem"] = it["eng"] + str(rr_cnt[it["eng"]] % DMA_RR[it["eng"]])
                rr_cnt[it["eng"]] += 1
            else:
                it["sem"] = it["eng"]
        milestones = {it["opid"] for it in self.items if it["dma"]}
        for it in self.items:
            milestones |= it["deps"]
        sem_names = sorted({self.items[m]["sem"] for m in milestones})
        sems = {n: st.enter_context(nc.semaphore("sem_" + n)) for n in sem_names}
        cum = {n: 0 for n in sem_names}
        val = {}
        for it in self.items:
            if it["opid"] in milestones:
                cum[it["sem"]] += 16 if it["dma"] else 1
                val[it["opid"]] = (it["sem"], cum[it["sem"]])
        per_eng = {}
        for it in self.items:
            per_eng.setdefault(it["eng"], []).append(it)

        def make_body(items):
            def body(e):
                waited = {}
                for it in items:
                    wl = {}
                    for dep in it["deps"]:
                        sname, v = val[dep]
                        if wl.get(sname, 0) < v:
                            wl[sname] = v
                    for sname in sorted(wl):
                        v = wl[sname]
                        if waited.get(sname, 0) < v:
                            e.wait_ge(sems[sname], v)
                            waited[sname] = v
                    inst = it["emit"](e)
                    if it["opid"] in milestones:
                        if it["dma"]:
                            inst.then_inc(sems[it["sem"]], 16)
                        elif it.get("nopinc"):
                            e.maybe_nop_then_inc((sems[it["sem"]], 1))
                        else:
                            e.maybe_drain_then_inc((sems[it["sem"]], 1), fusable=True)
            return body

        for eng, items in per_eng.items():
            getattr(blk, BLOCKNAME[eng])(make_body(items))


def build_program(phases=(1, 2, 3, 4)):
    nc = bass.Bass("TRN2", target_bir_lowering=False)
    dX = nc.dram_tensor("xo", [128, DC, NT], F32, kind="ExternalInput").ap()
    dXh = nc.dram_tensor("xh", [128, DC, HALO], F32, kind="ExternalInput").ap()
    dCos = nc.dram_tensor("cosb", [128, HALO + NT], BF16, kind="ExternalInput").ap()
    dSin = nc.dram_tensor("sinb", [128, HALO + NT], BF16, kind="ExternalInput").ap()
    dAmask = nc.dram_tensor("amask", [128, 8, 640], BF16, kind="ExternalInput").ap()
    dWq = nc.dram_tensor("wq_t", [16, 128, D], F32, kind="ExternalInput").ap()
    dWk = nc.dram_tensor("wk_t", [4, 128, D], F32, kind="ExternalInput").ap()
    dWv = nc.dram_tensor("wv", [D, 512], F32, kind="ExternalInput").ap()
    dWo = nc.dram_tensor("wo_t", [16, 128, D], F32, kind="ExternalInput").ap()
    dLam = nc.dram_tensor("lam_bc", [128, 16], F32, kind="ExternalInput").ap()
    dWqb = nc.dram_tensor("wqb_t", [16, 128, D], F32, kind="ExternalInput").ap()
    dWout = nc.dram_tensor("wout_t", [16, 128, D], F32, kind="ExternalInput").ap()
    dPmk = nc.dram_tensor("pmk_bd", [128, 16, 32], F32, kind="ExternalInput").ap()
    dPmv = nc.dram_tensor("pmv_bd", [32, 16, 128], F32, kind="ExternalInput").ap()
    dBmask = nc.dram_tensor("bmask", [128, 512], F32, kind="ExternalInput").ap()
    dWn = nc.dram_tensor("wn_fm", [128, DC], F32, kind="ExternalInput").ap()
    dWg = nc.dram_tensor("wg_t", [HC, 128, D], F32, kind="ExternalInput").ap()
    dWu = nc.dram_tensor("wu_t", [HC, 128, D], F32, kind="ExternalInput").ap()
    dWd = nc.dram_tensor("wd_t", [16, 128, 8192], F32, kind="ExternalInput").ap()
    dIdent = nc.dram_tensor("ident", [128, 128], BF16, kind="ExternalInput").ap()
    dY = nc.dram_tensor("y", [128, DC, NT], F32, kind="ExternalOutput").ap()
    dXspill = nc.dram_tensor("xspill", [128, DC, NT], F32, kind="Internal").ap()
    dWgS = nc.dram_tensor("wg_s", [HC, 128, D], BF16, kind="Internal").ap()
    dWuS = nc.dram_tensor("wu_s", [HC, 128, D], BF16, kind="Internal").ap()
    dWdS = nc.dram_tensor("wd_s", [16, 128, 8192], BF16, kind="Internal").ap()

    p = Prog()
    st = ExitStack()
    with st:
        xs = st.enter_context(nc.sbuf_tensor("xs", [128, DC, NT], F32))
        ident = st.enter_context(nc.sbuf_tensor("identb", [128, 128], BF16))
        ones_f = st.enter_context(nc.sbuf_tensor("ones_f", [128, 1], F32))
        ones_b = st.enter_context(nc.sbuf_tensor("ones_b", [128, 1], BF16))
        ones_r = st.enter_context(nc.sbuf_tensor("ones_r", [1, 128], F32))
        sqb = st.enter_context(nc.sbuf_tensor("sqb", [128, 2, 512], BF16))
        strow = st.enter_context(nc.sbuf_tensor("strow", [1, 3, 512], F32))
        ab_row = st.enter_context(nc.sbuf_tensor("ab_row", [1, 1024], F32))
        lamn = st.enter_context(nc.sbuf_tensor("lamn", [128, 16], F32))
        wnfm = st.enter_context(nc.sbuf_tensor("wnfm", [128, DC], F32))
        lam_t = st.enter_context(nc.sbuf_tensor("lam_t", [128, 16], F32))
        lntmp = st.enter_context(nc.sbuf_tensor("lntmp", [128, 2, 512], F32))
        epsv = st.enter_context(nc.sbuf_tensor("epsv", [128, 2], F32))

        def dma(eng, dst, src, reads=(), writes=()):
            return p.op(eng, (lambda e, d=dst, s=src: e.dma_start(out=d, in_=s)),
                        reads=reads, writes=writes, dma=True)

        dma("sy", ident[:], dIdent, writes=["ident"])
        dma("sy", wnfm[:], dWn, writes=["wnfm"])
        dma("sy", lam_t[:], dLam, writes=["lam"])
        p.op("dv", lambda e: e.tensor_scalar(out=lamn[:], in0=lam_t[:], scalar1=-1.0,
                                             scalar2=None, op0=ALU.mult),
             reads=["lam"], writes=["lamn"])
        p.op("dv", lambda e: e.memset(ones_f[:], 1.0), writes=["ones_f"])
        p.op("dv", lambda e: e.memset(ones_b[:], 1.0), writes=["ones_b"])
        p.op("dv", lambda e: e.memset(ones_r[:], 1.0), writes=["ones_r"])
        p.op("dv", lambda e: e.memset(epsv[:, 0:1], LN_EPS), writes=["eps0"])
        p.op("dv", lambda e: e.memset(epsv[:, 1:2], RMS_EPS), writes=["eps1"])
        for h in range(2):
            dma("sy", xs[:, :, h * 512:(h + 1) * 512], dX[:, :, h * 512:(h + 1) * 512],
                writes=[f"x{dc}c{h}" for dc in range(DC)])

        def xkeys(ch):
            return [f"x{dc}c{ch}" for dc in range(DC)]

        def tap(name, ap, shape, dtype, keys):
            if not DEBUG_TAPS:
                return
            dt_ = nc.dram_tensor(name, shape, dtype, kind="ExternalOutput").ap()
            dma("sy", dt_, ap, reads=keys, writes=[name])
            p.op("sy", lambda e: e.nop(), reads=[name])

        def emit_ln(tag, src, srckeys, dst, dstkeys, mode="plain", W=512):
            with nc.psum_tensor("pstat_" + tag, [1, 1024], F32) as pst, \
                 nc.psum_tensor("pbc_" + tag, [128, 1024], F32) as pbc, \
                 nc.sbuf_tensor("sqf_" + tag, [128, DC, W], BF16) as sqf:
                sA = ab_row[:, 0:W]
                sB = ab_row[:, 512:512 + W]
                s0 = strow[:, 0, 0:W]
                s1_ = strow[:, 1, 0:W]
                s2 = strow[:, 2, 0:W]

                def mm_x(e):
                    r = None
                    for dc in range(DC):
                        r = e.matmul(out=pst[:, 0:W], lhsT=ones_f[:],
                                     rhs=src[:, dc, :],
                                     start=(dc == 0), stop=(dc == DC - 1))
                    return r
                p.op("pe", mm_x, reads=list(srckeys) + ["ones_f"], writes=["pstatx"])
                p.op("ac", lambda e: e.activation(out=sqf[:], in_=src, func=AF.Square),
                     reads=list(srckeys), writes=["sqf"])

                def mm_q(e):
                    r = None
                    for dc in range(DC):
                        r = e.matmul(out=pst[:, 512:512 + W], lhsT=ones_b[:],
                                     rhs=sqf[:, dc, :],
                                     start=(dc == 0), stop=(dc == DC - 1))
                    return r
                p.op("pe", mm_q, reads=["sqf", "ones_b"], writes=["pstatq"])
                p.op("dv", lambda e: e.tensor_scalar(out=s0, in0=pst[0:1, 0:W],
                                                     scalar1=1.0 / D, scalar2=None,
                                                     op0=ALU.mult),
                     reads=["pstatx"], writes=["st0"])
                p.op("dv", lambda e: e.tensor_scalar(out=s1_, in0=pst[0:1, 512:512 + W],
                                                     scalar1=1.0 / D, scalar2=None,
                                                     op0=ALU.mult),
                     reads=["pstatq"], writes=["st1"])
                p.op("dv", lambda e: e.tensor_tensor(out=s2, in0=s0, in1=s0, op=ALU.mult),
                     reads=["st0"], writes=["st2"])
                p.op("dv", lambda e: e.tensor_tensor(out=s1_, in0=s1_, in1=s2,
                                                     op=ALU.subtract),
                     reads=["st1", "st2"], writes=["st1"])
                p.op("ac", lambda e: e.activation(out=s2, in_=s1_, func=AF.Ln,
                                                  bias=epsv[0:1, 0:1]),
                     reads=["st1", "eps0"], writes=["st2"])
                p.op("ac", lambda e: e.activation(out=sA, in_=s2, func=AF.Exp, scale=-0.5),
                     reads=["st2"], writes=["stA"])
                if mode == "rms":
                    p.op("dv", lambda e: e.tensor_tensor(out=s2, in0=sA, in1=sA,
                                                         op=ALU.mult),
                         reads=["stA"], writes=["st2"])
                    p.op("dv", lambda e: e.tensor_tensor(out=s2, in0=s1_, in1=s2,
                                                         op=ALU.mult),
                         reads=["st1", "st2"], writes=["st2"])
                    p.op("ac", lambda e: e.activation(out=s2, in_=s2, func=AF.Ln,
                                                      bias=epsv[0:1, 1:2]),
                         reads=["st2", "eps1"], writes=["st2"])
                    p.op("ac", lambda e: e.activation(out=s2, in_=s2, func=AF.Exp,
                                                      scale=-0.5),
                         reads=["st2"], writes=["st2"])
                    p.op("dv", lambda e: e.tensor_tensor(out=sA, in0=sA, in1=s2,
                                                         op=ALU.mult),
                         reads=["stA", "st2"], writes=["stA"])
                p.op("dv", lambda e: e.scalar_tensor_tensor(out=sB, in0=s0, scalar=-1.0,
                                                            in1=sA, op0=ALU.mult,
                                                            op1=ALU.mult),
                     reads=["st0", "stA"], writes=["stB"])

                def bcmm(e):
                    e.matmul(out=pbc[:, 0:512], lhsT=ones_r[:], rhs=ab_row[:, 0:512],
                             start=True, stop=True)
                    return e.matmul(out=pbc[:, 512:1024], lhsT=ones_r[:],
                                    rhs=ab_row[:, 512:1024], start=True, stop=True)
                p.op("pe", bcmm, reads=["stA", "stB", "ones_r"], writes=["pbc"])
                abc = pbc[:, 0:W].unsqueeze(1).broadcast_to((128, 2, W))
                bbc = pbc[:, 512:512 + W].unsqueeze(1).broadcast_to((128, 2, W))
                wnb = wnfm[:].unsqueeze(-1).broadcast_to((128, DC, W))
                for d2 in range(DC // 2):
                    sl = slice(2 * d2, 2 * d2 + 2)
                    p.op("dv", (lambda e, sl=sl, abc=abc: e.tensor_tensor(
                        out=lntmp[:, :, 0:W], in0=src[:, sl, :], in1=abc, op=ALU.mult)),
                        reads=[srckeys[2 * d2], srckeys[2 * d2 + 1], "pbc"],
                        writes=["lt"])
                    if mode == "rms":
                        p.op("dv", (lambda e, sl=sl, bbc=bbc: e.tensor_tensor(
                            out=lntmp[:, :, 0:W], in0=lntmp[:, :, 0:W], in1=bbc,
                            op=ALU.add)),
                            reads=["lt", "pbc"], writes=["lt"])
                        p.op("dv", (lambda e, sl=sl, wnb=wnb: e.tensor_tensor(
                            out=dst[:, sl, :], in0=lntmp[:, :, 0:W],
                            in1=wnb[:, sl, :], op=ALU.mult)),
                            reads=["lt", "wnfm"],
                            writes=[dstkeys[2 * d2], dstkeys[2 * d2 + 1]])
                    else:
                        p.op("dv", (lambda e, sl=sl, bbc=bbc: e.tensor_tensor(
                            out=dst[:, sl, :], in0=lntmp[:, :, 0:W], in1=bbc,
                            op=ALU.add)),
                            reads=["lt", "pbc"],
                            writes=[dstkeys[2 * d2], dstkeys[2 * d2 + 1]])
            p.barrier()

        def emit_proj_residual(tag, dW, srcbuf, srckey_fn):
            with nc.sbuf_tensor("wpb_" + tag, [128, 2, DC, 128], BF16) as wpb, \
                 nc.psum_tensor("pso_" + tag, [128, 2, 512], F32) as pso:
                for j in range(16):
                    bi = j % 2
                    dma("pl", wpb[:, bi, :, :],
                        dW[j].rearrange("p (dc o) -> p dc o", o=128),
                        writes=[f"wpb{tag}{bi}"])
                    def omm(e, bi=bi):
                        r = None
                        for fc in range(DC):
                            for ch in range(2):
                                r = e.matmul(out=pso[:, ch, :], lhsT=wpb[:, bi, fc, :],
                                             rhs=srcbuf[:, fc, ch * 512:(ch + 1) * 512],
                                             start=(fc == 0), stop=(fc == DC - 1))
                        return r
                    p.op("pe", omm,
                         reads=[f"wpb{tag}{bi}"] +
                               [srckey_fn(fc, ch) for fc in range(DC) for ch in range(2)],
                         writes=["pso0", "pso1"])
                    for ch in range(2):
                        p.op("dv", (lambda e, j=j, ch=ch: e.tensor_tensor(
                            out=xs[:, j, ch * 512:(ch + 1) * 512],
                            in0=pso[:, ch, :],
                            in1=xs[:, j, ch * 512:(ch + 1) * 512], op=ALU.add)),
                            reads=[f"pso{ch}", f"x{j}c{ch}"], writes=[f"x{j}c{ch}"])
            p.barrier()

        def emit_ffn(tag, make_scratch=False):
            for ch in range(2):
                mk = make_scratch and ch == 0
                sf = ExitStack()
                lnf = sf.enter_context(nc.sbuf_tensor(f"lnf{tag}{ch}", [128, DC, 512], BF16))
                emit_ln(f"{tag}c{ch}",
                        xs[:, :, ch * 512:(ch + 1) * 512], xkeys(ch),
                        lnf[:], [f"lnf_{dc}" for dc in range(DC)])
                gated = sf.enter_context(nc.sbuf_tensor(f"gated{tag}{ch}",
                                                        [128, HC, 512], BF16))
                with nc.sbuf_tensor(f"wgb{tag}{ch}", [128, 2, 2, DC, 128], BF16) as wgb, \
                     nc.sbuf_tensor(f"wub{tag}{ch}", [128, 2, 2, DC, 128], BF16) as wub, \
                     nc.sbuf_tensor(f"sgb{tag}{ch}", [128, 2, 2, 512], F32) as sgb, \
                     nc.psum_tensor(f"psg{tag}{ch}", [128, 2, 2, 512], F32) as psg, \
                     nc.psum_tensor(f"psu{tag}{ch}", [128, 2, 2, 512], F32) as psu:
                    def ldgu(g):
                        bi = g % 2
                        if mk:
                            dma("pl", wgb[:, bi, :, :, :],
                                dWg[2 * g:2 * g + 2]
                                .rearrange("h p (dc o) -> p h dc o", o=128),
                                writes=[f"wgb{bi}"])
                            dma("pl", wub[:, bi, :, :, :],
                                dWu[2 * g:2 * g + 2]
                                .rearrange("h p (dc o) -> p h dc o", o=128),
                                writes=[f"wub{bi}"])
                            dma("sy", dWgS[2 * g:2 * g + 2]
                                .rearrange("h p (dc o) -> p h dc o", o=128),
                                wgb[:, bi, :, :, :],
                                reads=[f"wgb{bi}"], writes=[f"wgscr{g}"])
                            dma("sy", dWuS[2 * g:2 * g + 2]
                                .rearrange("h p (dc o) -> p h dc o", o=128),
                                wub[:, bi, :, :, :],
                                reads=[f"wub{bi}"], writes=[f"wuscr{g}"])
                        else:
                            dma("sy", wgb[:, bi, :, :, :],
                                dWgS[2 * g:2 * g + 2]
                                .rearrange("h p (dc o) -> p h dc o", o=128),
                                reads=[f"wgscr{g}"], writes=[f"wgb{bi}"])
                            dma("sy", wub[:, bi, :, :, :],
                                dWuS[2 * g:2 * g + 2]
                                .rearrange("h p (dc o) -> p h dc o", o=128),
                                reads=[f"wuscr{g}"], writes=[f"wub{bi}"])
                    ldgu(0)
                    for g in range(HC // 2):
                        bi = g % 2
                        if g + 1 < HC // 2:
                            ldgu(g + 1)
                        def gumm(e, bi=bi):
                            r = None
                            for dc in range(DC):
                                for hh in range(2):
                                    e.matmul(out=psg[:, bi, hh, :],
                                             lhsT=wgb[:, bi, hh, dc, :],
                                             rhs=lnf[:, dc, :],
                                             start=(dc == 0), stop=(dc == DC - 1))
                                    r = e.matmul(out=psu[:, bi, hh, :],
                                                 lhsT=wub[:, bi, hh, dc, :],
                                                 rhs=lnf[:, dc, :],
                                                 start=(dc == 0), stop=(dc == DC - 1))
                            return r
                        p.op("pe", gumm,
                             reads=[f"wgb{bi}", f"wub{bi}"] +
                                   [f"lnf_{dc}" for dc in range(DC)],
                             writes=[f"psg{bi}", f"psu{bi}"])
                        p.op("ac", (lambda e, bi=bi: e.activation(
                            out=sgb[:, bi, :, :], in_=psg[:, bi, :, :], func=AF.Silu)),
                            reads=[f"psg{bi}"], writes=[f"sgb{bi}"])
                        p.op("dv", (lambda e, g2=g, bi=bi: e.tensor_tensor(
                            out=gated[:, 2 * g2:2 * g2 + 2, :], in0=sgb[:, bi, :, :],
                            in1=psu[:, bi, :, :], op=ALU.mult)),
                            reads=[f"sgb{bi}", f"psu{bi}"],
                            writes=[f"g{2 * g}", f"g{2 * g + 1}"])
                p.barrier()
                with nc.sbuf_tensor(f"wdb{tag}{ch}", [128, 2, HC, 128], BF16) as wdb, \
                     nc.psum_tensor(f"psd{tag}{ch}", [128, 2, 512], F32) as psd:
                    gkeys = [f"g{hc}" for hc in range(HC)]
                    for j in range(16):
                        bi = j % 2
                        for hhalf in range(2):
                            if mk:
                                dma("pl", wdb[:, bi, hhalf * 32:(hhalf + 1) * 32, :],
                                    dWd[j][:, hhalf * 4096:(hhalf + 1) * 4096]
                                    .rearrange("p (hc o) -> p hc o", o=128),
                                    writes=[f"wdb{bi}h{hhalf}"])
                                dma("sy", dWdS[j][:, hhalf * 4096:(hhalf + 1) * 4096]
                                    .rearrange("p (hc o) -> p hc o", o=128),
                                    wdb[:, bi, hhalf * 32:(hhalf + 1) * 32, :],
                                    reads=[f"wdb{bi}h{hhalf}"],
                                    writes=[f"wdscr{j}h{hhalf}"])
                            else:
                                dma("sy", wdb[:, bi, hhalf * 32:(hhalf + 1) * 32, :],
                                    dWdS[j][:, hhalf * 4096:(hhalf + 1) * 4096]
                                    .rearrange("p (hc o) -> p hc o", o=128),
                                    reads=[f"wdscr{j}h{hhalf}"],
                                    writes=[f"wdb{bi}h{hhalf}"])
                        def dmm(e, bi=bi):
                            r = None
                            for hc in range(HC):
                                r = e.matmul(out=psd[:, bi, :], lhsT=wdb[:, bi, hc, :],
                                             rhs=gated[:, hc, :],
                                             start=(hc == 0), stop=(hc == HC - 1))
                            return r
                        p.op("pe", dmm,
                             reads=[f"wdb{bi}h0", f"wdb{bi}h1"] + gkeys,
                             writes=[f"psd{bi}"])
                        p.op("dv", (lambda e, j=j, ch=ch, bi=bi: e.tensor_tensor(
                            out=xs[:, j, ch * 512:(ch + 1) * 512], in0=psd[:, bi, :],
                            in1=xs[:, j, ch * 512:(ch + 1) * 512], op=ALU.add)),
                            reads=[f"psd{bi}", f"x{j}c{ch}"], writes=[f"x{j}c{ch}"])
                sf.close()
                p.barrier()

        def emit_attn_b():
            s3 = ExitStack()
            qbfm = s3.enter_context(nc.sbuf_tensor("qbfm", [128, DC, NT], BF16))
            with nc.sbuf_tensor("yb3", [128, DC, NT], BF16) as yb:
                for ch in range(2):
                    emit_ln("s3c%d" % ch,
                            xs[:, :, ch * 512:(ch + 1) * 512], xkeys(ch),
                            yb[:, :, ch * 512:(ch + 1) * 512],
                            [f"y3_{dc}c{ch}" for dc in range(DC)], mode="rms")
                with nc.sbuf_tensor("wqbb", [128, 2, DC, 128], BF16) as wqbb, \
                     nc.psum_tensor("psq2", [128, 2, 512], F32) as psq2:
                    for j in range(16):
                        bi = j % 2
                        dma("pl", wqbb[:, bi, :, :],
                            dWqb[j].rearrange("p (dc o) -> p dc o", o=128),
                            writes=[f"wqbb{bi}"])
                        for cc in range(2):
                            def q2mm(e, bi=bi, cc=cc):
                                r = None
                                for dc in range(DC):
                                    r = e.matmul(out=psq2[:, cc, :], lhsT=wqbb[:, bi, dc, :],
                                                 rhs=yb[:, dc, cc * 512:(cc + 1) * 512],
                                                 start=(dc == 0), stop=(dc == DC - 1))
                                return r
                            p.op("pe", q2mm,
                                 reads=[f"wqbb{bi}"] + [f"y3_{dc}c{cc}" for dc in range(DC)],
                                 writes=[f"psq2{cc}"])
                            p.op("dv", (lambda e, j=j, cc=cc: e.tensor_copy(
                                out=qbfm[:, j, cc * 512:(cc + 1) * 512], in_=psq2[:, cc, :])),
                                reads=[f"psq2{cc}"], writes=[f"qb{j}c{cc}"])
            p.barrier()
            ob_fm = s3.enter_context(nc.sbuf_tensor("ob_fm", [128, DC, NT], BF16))
            s3b = ExitStack()
            pmkb = s3b.enter_context(nc.sbuf_tensor("pmkb", [128, 16, 32], BF16))
            pmvb = s3b.enter_context(nc.sbuf_tensor("pmvb", [32, 16, 128], BF16))
            bmaskb = s3b.enter_context(nc.sbuf_tensor("bmaskb", [128, 512], F32))
            smf = s3b.enter_context(nc.sbuf_tensor("smf", [128, 2, 512], F32))
            ebf = s3b.enter_context(nc.sbuf_tensor("ebf", [128, 2, 512], F32))
            smx = s3b.enter_context(nc.sbuf_tensor("smx", [128, 2, 32], F32))
            sms = s3b.enter_context(nc.sbuf_tensor("sms", [128, 2, 32], F32))
            pb2 = s3b.enter_context(nc.sbuf_tensor("pb2", [128, 2, 512], BF16))
            ptb = s3b.enter_context(nc.sbuf_tensor("ptb", [32, 2, 16, 128], BF16))
            dma("pl", pmkb[:], dPmk, writes=["pmkb"])
            dma("pl", pmvb[:], dPmv, writes=["pmvb"])
            dma("sy", bmaskb[:], dBmask, writes=["bmaskb"])
            with nc.psum_tensor("psS", [128, 2, 512], F32) as psS, \
                 nc.psum_tensor("psT2", [32, 16, 128], BF16) as psT2, \
                 nc.psum_tensor("psV2", [128, 2048], F32) as psV2:

                def b_smm(t):
                    a2 = t % 2
                    tc = t * 128

                    def bsmm(e, a2=a2, tc=tc):
                        r = None
                        for j in range(16):
                            r = e.matmul(out=psS[:, a2, j * 32:(j + 1) * 32],
                                         lhsT=qbfm[:, j, tc:tc + 128],
                                         rhs=pmkb[:, j, :], start=True, stop=True)
                        return r
                    p.op("pe", bsmm,
                         reads=["pmkb"] + [f"qb{j}c{t // 4}" for j in range(16)],
                         writes=[f"psS{a2}"])

                def b_chain(t):
                    a2 = t % 2
                    smfv = smf[:, a2, :]
                    smfk = f"smf{a2}"
                    ebfv = ebf[:, a2, :]
                    ebfk = f"ebf{a2}"
                    smxv = smx[:, a2, :]
                    smxk = f"smx{a2}"
                    smsv = sms[:, a2, :]
                    smsk = f"sms{a2}"
                    pb2v = pb2[:, a2, :]
                    pb2k = f"pb2{a2}"
                    p.op("dv", (lambda e, a2=a2, smfv=smfv: e.tensor_tensor(
                        out=smfv, in0=psS[:, a2, :], in1=bmaskb[:], op=ALU.add)),
                        reads=[f"psS{a2}", "bmaskb"], writes=[smfk])
                    smf3 = smfv.rearrange("p (h k) -> p h k", k=16)
                    p.op("dv", (lambda e, smf3=smf3, smxv=smxv: e.tensor_reduce(
                        out=smxv, in_=smf3, axis=AX.X, op=ALU.max)),
                        reads=[smfk], writes=[smxk])
                    p.op("dv", (lambda e, smf3=smf3, smxv=smxv: e.tensor_tensor(
                        out=smf3, in0=smf3,
                        in1=smxv.unsqueeze(-1).broadcast_to((128, 32, 16)),
                        op=ALU.subtract)),
                        reads=[smfk, smxk], writes=[smfk])
                    p.op("ac", (lambda e, smfv=smfv, ebfv=ebfv: e.activation(
                        out=ebfv, in_=smfv, func=AF.Exp, scale=0.125)),
                        reads=[smfk], writes=[ebfk])
                    ebf3 = ebfv.rearrange("p (h k) -> p h k", k=16)
                    p.op("dv", (lambda e, ebf3=ebf3, smsv=smsv: e.tensor_reduce(
                        out=smsv, in_=ebf3, axis=AX.X, op=ALU.add)),
                        reads=[ebfk], writes=[smsk])
                    p.op("ac", (lambda e, smsv=smsv, smxv=smxv: e.activation(
                        out=smxv, in_=smsv, func=AF.Ln)),
                        reads=[smsk], writes=[smxk + "l"])
                    p.op("ac", (lambda e, smxv=smxv: e.activation(
                        out=smxv, in_=smxv, func=AF.Exp, scale=-1.0)),
                        reads=[smxk + "l"], writes=[smxk])
                    pb23 = pb2v.rearrange("p (h k) -> p h k", k=16)
                    p.op("dv", (lambda e, ebf3=ebf3, pb23=pb23, smxv=smxv:
                                e.tensor_tensor(
                        out=pb23, in0=ebf3,
                        in1=smxv.unsqueeze(-1).broadcast_to((128, 32, 16)),
                        op=ALU.mult)),
                        reads=[ebfk, smxk], writes=[pb2k])

                def b_tail(t):
                    a2 = t % 2
                    tc = t * 128
                    pb2v = pb2[:, a2, :]
                    pb2k = f"pb2{a2}"
                    ptbv = ptb[:, a2, :, :]
                    ptbk = f"ptb{a2}"

                    def btr(e, pb2v=pb2v, ptbv=ptbv):
                        r = None
                        for j in range(16):
                            r = e.transpose(out=psT2[:, j, :],
                                            in_=pb2v[:, j * 32:(j + 1) * 32],
                                            identity=ident[:])
                        return r
                    p.op("pe", btr, reads=[pb2k, "ident"], writes=["psT2"])
                    p.op("dv", (lambda e, ptbv=ptbv: e.tensor_copy(
                        out=ptbv, in_=psT2[:])),
                        reads=["psT2"], writes=[ptbk])

                    def bpv(e, ptbv=ptbv):
                        r = None
                        for j in range(16):
                            r = e.matmul(out=psV2[:, j * 128:(j + 1) * 128],
                                         lhsT=pmvb[:, j, :], rhs=ptbv[:, j, :],
                                         start=True, stop=True)
                        return r
                    p.op("pe", bpv, reads=[ptbk, "pmvb"], writes=["psV2"])
                    p.op("dv", (lambda e, tc=tc: e.tensor_copy(
                        out=ob_fm[:, :, tc:tc + 128],
                        in_=psV2[:].rearrange("p (j q) -> p j q", q=128))),
                        reads=["psV2"],
                        writes=[f"ob{j}c{t // 4}" for j in range(16)])

                for t in range(8):
                    b_smm(t)
                    if t >= 1:
                        b_tail(t - 1)
                    b_chain(t)
                b_tail(7)
            p.barrier()
            s3b.close()
            p.barrier()
            emit_proj_residual("wo3", dWout, ob_fm, (lambda fc, ch: f"ob{fc}c{ch}"))
            s3.close()
            p.barrier()

        if 1 in phases:
            s1 = ExitStack()
            qrot = s1.enter_context(nc.sbuf_tensor("qrot", [128, NPAIR, 2, NT], BF16))
            kbuf = s1.enter_context(nc.sbuf_tensor("kbuf", [128, 2, 2, HALO + NT], BF16))
            vbuf = s1.enter_context(nc.sbuf_tensor("vbuf", [128, NSLOT, 512], BF16))
            ln1 = s1.enter_context(nc.sbuf_tensor("ln1", [128, DC, NT], BF16))
            o_fm = ln1
            s1mid = ExitStack()
            lnh = s1mid.enter_context(nc.sbuf_tensor("lnh", [128, DC, HALO], BF16))
            for ch in range(2):
                emit_ln("s1o%d" % ch,
                        xs[:, :, ch * 512:(ch + 1) * 512], xkeys(ch),
                        ln1[:, :, ch * 512:(ch + 1) * 512],
                        [f"ln1_{dc}c{ch}" for dc in range(DC)])
            xs_flat = xs[:].rearrange("p a b -> p (a b)")

            def carve(off, shape, dtype):
                n = 1
                for d in shape[1:]:
                    n *= d
                if dtype == F32:
                    v = xs_flat[:, off:off + n]
                else:
                    v = xs_flat[:, off:off + (n + 1) // 2].bitcast(BF16)
                if len(shape) == 2:
                    return v
                pat = "p (" + " ".join(chr(97 + k) for k in range(len(shape) - 1)) + ") -> p " + \
                      " ".join(chr(97 + k) for k in range(len(shape) - 1))
                kw = {chr(97 + k): shape[k + 1] for k in range(len(shape) - 1)}
                return v.rearrange(pat, **kw)

            for h in range(2):
                dma("sy", dXspill[:, :, h * 512:(h + 1) * 512],
                    xs[:, :, h * 512:(h + 1) * 512],
                    reads=xkeys(h), writes=[f"xsp{h}"])
            p.barrier()
            xh = carve(0, [128, DC, HALO], F32)
            dma("sy", xh, dXh, writes=[f"xh{dc}" for dc in range(DC)])
            emit_ln("s1h", xh, [f"xh{dc}" for dc in range(DC)],
                    lnh[:], [f"lnh{dc}" for dc in range(DC)], W=512)
            p.barrier()

            tap("dbg_ln1", ln1[:], [128, DC, NT], BF16,
                [f"ln1_{dc}c{ch}" for dc in range(DC) for ch in range(2)])
            tap("dbg_lnh", lnh[:], [128, DC, HALO], BF16,
                [f"lnh{dc}h{hh}" for dc in range(DC) for hh in range(4)])
            s1r = ExitStack()
            cosb = s1r.enter_context(nc.sbuf_tensor("cosb_s", [128, HALO + NT], BF16))
            sinb = s1r.enter_context(nc.sbuf_tensor("sinb_s", [128, HALO + NT], BF16))
            dma("sy", cosb[:], dCos, writes=["cos"])
            dma("sy", sinb[:], dSin, writes=["sin"])
            rc0 = carve(0, [128, 2, 512], F32)
            rtA = carve(1024, [128, 2, 512], F32)

            def emit_rope(psq, kps, tabcol, out0, out1, k0, k1, ch=0):
                cs = cosb[:, tabcol:tabcol + 512]
                sn = sinb[:, tabcol:tabcol + 512]
                rc = rc0[:, ch, :]
                rt = rtA[:, ch, :]
                kr = kps + "rc%d" % ch
                kt = kps + "rt%d" % ch
                ks = kps + "s0%d" % ch
                p.op("dv", (lambda e, psq=psq, rc=rc: e.tensor_copy(
                    out=rc, in_=psq[:, 0:512])),
                    reads=[kps], writes=[kr])
                p.op("dv", (lambda e, psq=psq, sn=sn, rt=rt: e.tensor_tensor(
                    out=rt, in0=psq[:, 512:1024], in1=sn, op=ALU.mult)),
                    reads=[kps, "sin"], writes=[kt])
                p.op("dv", (lambda e, psq=psq, cs=cs, rc=rc: e.tensor_tensor(
                    out=psq[:, 0:512], in0=rc, in1=cs, op=ALU.mult)),
                    reads=[kr, "cos", kps], writes=[ks])
                p.op("dv", (lambda e, psq=psq, out0=out0, rt=rt: e.tensor_tensor(
                    out=out0, in0=psq[:, 0:512], in1=rt, op=ALU.subtract)),
                    reads=[ks, kt], writes=[k0])
                p.op("dv", (lambda e, psq=psq, cs=cs, rt=rt: e.tensor_tensor(
                    out=rt, in0=psq[:, 512:1024], in1=cs, op=ALU.mult)),
                    reads=[kps, "cos", kt], writes=[kt])
                p.op("dv", (lambda e, psq=psq, sn=sn, rc=rc: e.tensor_tensor(
                    out=psq[:, 0:512], in0=rc, in1=sn, op=ALU.mult)),
                    reads=[kr, "sin", ks], writes=[ks])
                p.op("dv", (lambda e, psq=psq, out1=out1, rt=rt: e.tensor_tensor(
                    out=out1, in0=rt, in1=psq[:, 0:512], op=ALU.add)),
                    reads=[kt, ks], writes=[k1])

            with nc.sbuf_tensor("wqb", [128, 2, 2, DC, 128], BF16) as wqb, \
                 nc.psum_tensor("psq", [128, 2, 2, 2, 512], F32) as psq:
                for i in range(NPAIR):
                    bi = i % 2
                    for s in range(2):
                        dma("pl", wqb[:, bi, s, :, :],
                            dWq[2 * i + s].rearrange("p (dc o) -> p dc o", o=128),
                            writes=[f"wqb{bi}{s}"])
                    def qmm(e, bi=bi):
                        r = None
                        for s in range(2):
                            for dc in range(DC):
                                for ch in range(2):
                                    r = e.matmul(out=psq[:, bi, ch, s, :],
                                                 lhsT=wqb[:, bi, s, dc, :],
                                                 rhs=ln1[:, dc, ch * 512:(ch + 1) * 512],
                                                 start=(dc == 0), stop=(dc == DC - 1))
                        return r
                    p.op("pe", qmm,
                         reads=[f"wqb{bi}0", f"wqb{bi}1"] +
                               [f"ln1_{dc}c{ch}" for dc in range(DC) for ch in range(2)],
                         writes=[f"psq{bi}", f"psq{bi}s00", f"psq{bi}s01"])
                    for ch in range(2):
                        emit_rope(psq[:, bi, ch, :, :].rearrange("p a b -> p (a b)"),
                                  f"psq{bi}", HALO + ch * 512,
                                  qrot[:, i, 0, ch * 512:(ch + 1) * 512],
                                  qrot[:, i, 1, ch * 512:(ch + 1) * 512],
                                  f"q{i}_0c{ch}", f"q{i}_1c{ch}", ch=ch)
            p.barrier()

            tap("dbg_qrot", qrot[:], [128, NPAIR, 2, NT], BF16,
                [f"q{i}_{s}c{ch}" for i in range(NPAIR) for s in range(2) for ch in range(2)])
            lnsrc = [((lambda dc: lnh[:, dc, :]), [f"lnh{dc}" for dc in range(DC)]),
                     ((lambda dc: ln1[:, dc, 0:512]), [f"ln1_{dc}c0" for dc in range(DC)]),
                     ((lambda dc: ln1[:, dc, 512:1024]),
                      [f"ln1_{dc}c1" for dc in range(DC)])]
            with nc.sbuf_tensor("wkb", [128, 2, DC, 128], BF16) as wkb, \
                 nc.psum_tensor("psk", [128, 3, 2, 512], F32) as psk:
                allk = [k for _, kk in lnsrc for k in kk]
                for g in range(2):
                    for s in range(2):
                        dma("pl", wkb[:, s, :, :],
                            dWk[2 * g + s].rearrange("p (dc o) -> p dc o", o=128),
                            writes=[f"wkb{s}"])
                    def kmm(e):
                        r = None
                        for s in range(2):
                            for dc in range(DC):
                                for c3 in range(3):
                                    r = e.matmul(out=psk[:, c3, s, :],
                                                 lhsT=wkb[:, s, dc, :],
                                                 rhs=lnsrc[c3][0](dc),
                                                 start=(dc == 0), stop=(dc == DC - 1))
                        return r
                    p.op("pe", kmm, reads=["wkb0", "wkb1"] + allk,
                         writes=["psk", "psks00", "psks01", "psks02"])
                    for c3 in range(3):
                        emit_rope(psk[:, c3, :, :].rearrange("p a b -> p (a b)"),
                                  "psk", c3 * 512,
                                  kbuf[:, g, 0, c3 * 512:(c3 + 1) * 512],
                                  kbuf[:, g, 1, c3 * 512:(c3 + 1) * 512],
                                  f"kb{g}_0c{c3}", f"kb{g}_1c{c3}", ch=c3 % 2)
            s1r.close()
            p.barrier()
            tap("dbg_kbuf", kbuf[:], [128, 2, 2, HALO + NT], BF16,
                [f"kb{g}_{s}c{c3}" for g in range(2) for s in range(2) for c3 in range(3)])

            with nc.sbuf_tensor("wvb", [128, DC, 512], BF16) as wvb, \
                 nc.psum_tensor("psv", [128, 512], F32) as psv:
                dma("pl", wvb[:], dWv.rearrange("(dc p) n -> p dc n", p=128), writes=["wvb"])
                for t in range(NSLOT):
                    if t < 4:
                        lhsf = (lambda dc, t=t: lnh[:, dc, t * 128:(t + 1) * 128])
                        lkeys = [f"lnh{dc}" for dc in range(DC)]
                    else:
                        lhsf = (lambda dc, t=t: ln1[:, dc, (t - 4) * 128:(t - 3) * 128])
                        lkeys = [f"ln1_{dc}c{(t - 4) // 4}" for dc in range(DC)]
                    def vmm(e, lhsf=lhsf):
                        r = None
                        for dc in range(DC):
                            r = e.matmul(out=psv[:], lhsT=lhsf(dc), rhs=wvb[:, dc, :],
                                         start=(dc == 0), stop=(dc == DC - 1))
                        return r
                    p.op("pe", vmm, reads=lkeys + ["wvb"], writes=["psv"])
                    p.op("dv", (lambda e, t=t: e.tensor_copy(out=vbuf[:, t, :], in_=psv[:])),
                         reads=["psv"], writes=[f"vb{t}"])
            s1mid.close()
            p.barrier()
            tap("dbg_vbuf", vbuf[:], [128, NSLOT, 512], BF16,
                [f"vb{t}" for t in range(NSLOT)])

            s1b = ExitStack()
            amask = s1b.enter_context(nc.sbuf_tensor("amask_s", [128, 8, 640], BF16))
            dma("sy", amask[:], dAmask, writes=["amask"])
            amb_v = carve(0, [128, 4, 2, 640], F32)
            eb_v = carve(5120, [128, 4, 2, 640], F32)
            tb_v = carve(10240, [128, 2, 640], F32)
            pbuf_v = carve(11520, [128, 4, 640], BF16)
            pts_v = carve(12800, [128, 4, 640], BF16)
            sts_v = carve(14080, [128, 4, 14], F32)

            with nc.psum_tensor("psA_", [128, 1280], F32) as psA_, \
                 nc.psum_tensor("psB_", [128, 1280], F32) as psB_, \
                 nc.psum_tensor("psT", [128, 640], BF16) as psT, \
                 nc.psum_tensor("psO", [128, 256], F32) as psO:
                units = []
                for i in range(NPAIR):
                    g = i // 4
                    hA = i if i < 4 else i + 4
                    for qi in range(8):
                        for half in range(2):
                            units.append((i, g, hA, qi, half))

                def emit_smm(i, g, qi):
                    qc = qi * 128
                    qkeys = [f"q{i}_{s}c{qi // 4}" for s in range(2)]
                    kkeys = sorted({f"kb{g}_{s}c{(qi + rr) // 4}"
                                    for s in range(2) for rr in range(5)})

                    def smm(e, i=i, g=g, qi=qi, qc=qc):
                        r = None
                        for s in range(2):
                            for rr in range(5):
                                slot = qi + rr
                                o0 = s * 640 + rr * 128
                                for half, ps in ((0, psA_), (1, psB_)):
                                    lo = 0 if half == 0 else 64
                                    tp = None if half == 0 else (64, 0)
                                    r = e.matmul(
                                        out=ps[:, o0:o0 + 128],
                                        lhsT=qrot[lo:lo + 64, i, s, qc:qc + 128],
                                        rhs=kbuf[lo:lo + 64, g, s,
                                                 slot * 128:slot * 128 + 128],
                                        start=True, stop=True, tile_position=tp)
                        return r
                    p.op("pe", smm, reads=qkeys + kkeys, writes=["psA", "psB"])

                def s1_dv(k):
                    i, g, hA, qi, half = units[k]
                    cu = k % 4
                    sv = sts_v[:, cu, :]
                    sk = f"sts{cu}_"
                    kp = "psA" if half == 0 else "psB"
                    psH = psA_ if half == 0 else psB_
                    am2v = amb_v[:, cu, :, :]
                    amk = f"am{cu}"
                    p.op("dv", (lambda e, qi=qi, am2v=am2v, psH=psH: e.tensor_tensor(
                        out=am2v, in0=psH[:].rearrange("p (a k) -> p a k", k=640),
                        in1=amask[:, qi, :].unsqueeze(1).broadcast_to((128, 2, 640)),
                        op=ALU.add)),
                        reads=[kp, "amask"], writes=[amk])
                    p.op("dv", (lambda e, sv=sv, am2v=am2v: e.tensor_reduce(
                        out=sv[:, 0:2], in_=am2v, axis=AX.X, op=ALU.max)),
                        reads=[amk], writes=[sk + "m"])
                    p.op("dv", (lambda e, sv=sv: e.tensor_scalar(
                        out=sv[:, 2:4], in0=sv[:, 0:2], scalar1=-0.125,
                        scalar2=None, op0=ALU.mult)),
                        reads=[sk + "m"], writes=[sk + "b"])

                def s2_ac(k):
                    i, g, hA, qi, half = units[k]
                    cu = k % 4
                    sv = sts_v[:, cu, :]
                    sk = f"sts{cu}_"
                    am2v = amb_v[:, cu, :, :]
                    amk = f"am{cu}"
                    ebv = eb_v[:, cu, :, :]
                    ebk = f"eb{cu}"
                    p.op("ac", (lambda e, sv=sv, am2v=am2v, ebv=ebv: e.activation(
                        out=ebv[:, 0, :], in_=am2v[:, 0, :], func=AF.Exp,
                        bias=sv[:, 2:3], scale=0.125, accum_out=sv[:, 4:5])),
                        reads=[amk, sk + "b"], writes=[ebk + "1", sk + "s1"])
                    p.op("ac", (lambda e, sv=sv, am2v=am2v, ebv=ebv: e.activation(
                        out=ebv[:, 1, :], in_=am2v[:, 1, :], func=AF.Exp,
                        bias=sv[:, 3:4], scale=0.125, accum_out=sv[:, 5:6])),
                        reads=[amk, sk + "b"], writes=[ebk + "2", sk + "s2"])
                    p.op("ac", (lambda e, sv=sv: e.copy(
                        out=sv[:, 12:14], in_=sv[:, 4:6])),
                        reads=[sk + "s1", sk + "s2"],
                        writes=[sk + "s1f", sk + "s2f"])
                    p.op("ac", (lambda e, sv=sv: e.activation(
                        out=sv[:, 6:7], in_=sv[:, 13:14], func=AF.Ln)),
                        reads=[sk + "s2f"], writes=[sk + "lr2"])
                    p.op("ac", (lambda e, sv=sv: e.activation(
                        out=sv[:, 6:7], in_=sv[:, 6:7], func=AF.Exp, scale=-1.0)),
                        reads=[sk + "lr2"], writes=[sk + "r2"])

                def s3_dv(k):
                    i, g, hA, qi, half = units[k]
                    cu = k % 4
                    c2 = k % 2
                    sv = sts_v[:, cu, :]
                    sk = f"sts{cu}_"
                    ebv = eb_v[:, cu, :, :]
                    ebk = f"eb{cu}"
                    tbv = tb_v[:, c2, :]
                    tbk = f"tb{c2}"
                    h = hA + 4 * half
                    p.op("dv", (lambda e, sv=sv: e.tensor_tensor(
                        out=sv[:, 7:8], in0=sv[:, 12:13], in1=sv[:, 6:7],
                        op=ALU.mult)),
                        reads=[sk + "s1f", sk + "r2"], writes=[sk + "c"])
                    p.op("dv", (lambda e, sv=sv, h=h: e.tensor_scalar(
                        out=sv[:, 8:9], in0=sv[:, 7:8],
                        scalar1=lamn[:, h:h + 1], scalar2=None, op0=ALU.mult)),
                        reads=[sk + "c", "lamn"], writes=[sk + "cn"])
                    p.op("dv", (lambda e, sv=sv, ebv=ebv, tbv=tbv:
                                e.scalar_tensor_tensor(
                        out=tbv, in0=ebv[:, 1, :], scalar=sv[:, 8:9],
                        in1=ebv[:, 0, :], op0=ALU.mult, op1=ALU.add)),
                        reads=[ebk + "1", ebk + "2", sk + "cn"],
                        sem_reads=[sk + "s1f"], writes=[tbk])
                    p.op("dv", (lambda e, sv=sv, tbv=tbv, cu=cu: e.tensor_scalar(
                        out=amb_v[:, cu, 0, :], in0=tbv, scalar1=0.0, scalar2=0.0,
                        op0=ALU.max, op1=ALU.add, accum_out=sv[:, 9:10])),
                        reads=[tbk], writes=[f"am{cu}", sk + "S"])
                    p.op("dv", (lambda e, sv=sv: e.scalar_tensor_tensor(
                        out=sv[:, 10:11], in0=sv[:, 12:13], scalar=1e-6,
                        in1=sv[:, 9:10], op0=ALU.mult, op1=ALU.add)),
                        reads=[sk + "s1f"], sem_reads=[sk + "S"], writes=[sk + "dn"])

                def s4_ac(k):
                    i, g, hA, qi, half = units[k]
                    cu = k % 4
                    c2 = k % 2
                    sv = sts_v[:, cu, :]
                    sk = f"sts{cu}_"
                    tbv = tb_v[:, c2, :]
                    tbk = f"tb{c2}"
                    pbv = pbuf_v[:, cu, :]
                    p.op("ac", (lambda e, sv=sv: e.activation(
                        out=sv[:, 11:12], in_=sv[:, 10:11], func=AF.Ln)),
                        reads=[sk + "dn"], writes=[sk + "lr"])
                    p.op("ac", (lambda e, sv=sv: e.activation(
                        out=sv[:, 11:12], in_=sv[:, 11:12], func=AF.Exp,
                        scale=-1.0)),
                        reads=[sk + "lr"], writes=[sk + "r"])
                    p.op("ac", (lambda e, sv=sv, tbv=tbv, pbv=pbv: e.activation(
                        out=pbv, in_=tbv, func=AF.Relu, scale=sv[:, 11:12])),
                        reads=[tbk], sem_reads=[sk + "r"], writes=[f"pb{cu}"])

                def s5_tail(k):
                    i, g, hA, qi, half = units[k]
                    cu = k % 4

                    def trgpv(e, k=k, cu=cu):
                        r = None
                        for rr in range(5):
                            r = e.transpose(out=psT[:, rr * 128:(rr + 1) * 128],
                                            in_=pbuf_v[:, cu, rr * 128:(rr + 1) * 128],
                                            identity=ident[:])
                        if k >= 1:
                            ip, gp, hAp, qip, halfp = units[k - 1]
                            cup = (k - 1) % 4
                            kvh = 2 * gp + halfp
                            for rr in range(5):
                                slot = qip + rr
                                r = e.matmul(
                                    out=psO[:, halfp * 128:halfp * 128 + 128],
                                    lhsT=vbuf[:, slot, kvh * 128:kvh * 128 + 128],
                                    rhs=pts_v[:, cup, rr * 128:(rr + 1) * 128],
                                    start=(rr == 0), stop=(rr == 4))
                        return r
                    rd = [f"pb{cu}", "ident"]
                    wr = ["psT"]
                    if k >= 1:
                        ip, gp, hAp, qip, halfp = units[k - 1]
                        rd += [f"pts{(k - 1) % 4}"] + [f"vb{qip + rr}" for rr in range(5)]
                        wr += [f"psO{halfp}"]
                    p.op("pe", trgpv, reads=rd, writes=wr)
                    p.op("dv", (lambda e, cu=cu: e.tensor_copy(
                        out=pts_v[:, cu, :], in_=psT[:])),
                        reads=["psT"], writes=[f"pts{cu}"])
                    if k >= 1:
                        ip, gp, hAp, qip, halfp = units[k - 1]
                        if halfp == 1:
                            p.op("dv", (lambda e, hAp=hAp, qip=qip: e.tensor_copy(
                                out=o_fm[:, hAp:hAp + 5:4, qip * 128:qip * 128 + 128],
                                in_=psO[:].rearrange("p (h q) -> p h q", q=128))),
                                reads=["psO0", "psO1"],
                                writes=[f"of{hAp}c{qip // 4}", f"of{hAp + 4}c{qip // 4}"])

                def s6_last():
                    k = len(units) - 1
                    i, g, hA, qi, half = units[k]
                    cu = k % 4
                    kvh = 2 * g + half

                    def pvg(e, qi=qi, cu=cu, kvh=kvh, half=half):
                        r = None
                        for rr in range(5):
                            slot = qi + rr
                            r = e.matmul(
                                out=psO[:, half * 128:half * 128 + 128],
                                lhsT=vbuf[:, slot, kvh * 128:kvh * 128 + 128],
                                rhs=pts_v[:, cu, rr * 128:(rr + 1) * 128],
                                start=(rr == 0), stop=(rr == 4))
                        return r
                    p.op("pe", pvg,
                         reads=[f"pts{cu}"] + [f"vb{qi + rr}" for rr in range(5)],
                         writes=[f"psO{half}"])
                    p.op("dv", (lambda e, hA=hA, qi=qi: e.tensor_copy(
                        out=o_fm[:, hA:hA + 5:4, qi * 128:qi * 128 + 128],
                        in_=psO[:].rearrange("p (h q) -> p h q", q=128))),
                        reads=["psO0", "psO1"],
                        writes=[f"of{hA}c{qi // 4}", f"of{hA + 4}c{qi // 4}"])

                NU = len(units)
                for k in range(NU + 4):
                    if k < NU:
                        i, g, hA, qi, half = units[k]
                        if half == 0:
                            emit_smm(i, g, qi)
                        s1_dv(k)
                    if 0 <= k - 1 < NU:
                        s2_ac(k - 1)
                    if 0 <= k - 2 < NU:
                        s3_dv(k - 2)
                    if 0 <= k - 3 < NU:
                        s4_ac(k - 3)
                    if 0 <= k - 4 < NU:
                        s5_tail(k - 4)
                s6_last()
            p.barrier()
            s1b.close()
            p.barrier()
            for h in range(2):
                dma("sy", xs[:, :, h * 512:(h + 1) * 512],
                    dXspill[:, :, h * 512:(h + 1) * 512],
                    reads=[f"xsp{h}"], writes=xkeys(h))
            emit_proj_residual("wo1", dWo, o_fm, (lambda fc, ch: f"of{fc}c{ch}"))
            s1.close()
            p.barrier()

        if 2 in phases:
            emit_ffn("f2", make_scratch=(4 in phases or True))
        if 3 in phases:
            emit_attn_b()
        if 4 in phases:
            emit_ffn("f4")

        for h in range(2):
            dma("sy", dY[:, :, h * 512:(h + 1) * 512], xs[:, :, h * 512:(h + 1) * 512],
                reads=xkeys(h), writes=[f"yst{h}"])
        p.op("sy", lambda e: e.nop(), reads=["yst0", "yst1"])

        blk = st.enter_context(nc.Block())
        p.finalize_and_emit(nc, blk, st)
    return nc


def _tile_cols(W, colmaps):
    dcs = W.shape[0] // 128
    out = np.empty((len(colmaps), 128, W.shape[0]), dtype=np.float32)
    for j, cols in enumerate(colmaps):
        t = W[:, cols].reshape(dcs, 128, len(cols)).transpose(1, 0, 2)
        out[j] = t.reshape(128, dcs * len(cols))
    return out


def _tile_rowsT(W):
    nh = W.shape[0] // 128
    nd = W.shape[1] // 128
    out = np.empty((nh, 128, W.shape[1]), np.float32)
    for h in range(nh):
        blockT = W[h * 128:(h + 1) * 128, :].T
        out[h] = blockT.reshape(nd, 128, 128).transpose(1, 0, 2).reshape(128, W.shape[1])
    return out


def _fmify(rows):
    T = rows.shape[0]
    return np.ascontiguousarray(rows.reshape(T, DC, 128).transpose(2, 1, 0))


def kernel(L0_attn_a_x, L0_attn_a_Wq, L0_attn_a_Wk, L0_attn_a_Wv, L0_attn_a_Wo,
           L0_attn_a_lam, L0_attn_b_w_norm, L0_attn_b_w_qkv, L0_attn_b_w_out,
           L0_attn_b_pm, L0_ffn_w_gate, L0_ffn_w_up, L0_ffn_w_down):
    X = np.asarray(L0_attn_a_x, dtype=np.float32)
    Wq = np.asarray(L0_attn_a_Wq, dtype=np.float32)
    Wk = np.asarray(L0_attn_a_Wk, dtype=np.float32)
    Wv = np.ascontiguousarray(np.asarray(L0_attn_a_Wv, dtype=np.float32))
    Wo = np.asarray(L0_attn_a_Wo, dtype=np.float32)
    lam = np.asarray(L0_attn_a_lam, dtype=np.float32)
    wnorm = np.asarray(L0_attn_b_w_norm, dtype=np.float32)
    wqkv = np.asarray(L0_attn_b_w_qkv, dtype=np.float32)
    woutb = np.asarray(L0_attn_b_w_out, dtype=np.float32)
    pm = np.asarray(L0_attn_b_pm, dtype=np.float32)
    wg = np.asarray(L0_ffn_w_gate, dtype=np.float32)
    wu = np.asarray(L0_ffn_w_up, dtype=np.float32)
    wd = np.asarray(L0_ffn_w_down, dtype=np.float32)

    ar = np.arange(64)
    wq_maps = []
    for i in range(8):
        hA = i if i < 4 else i + 4
        for s in range(2):
            wq_maps.append(np.concatenate([hA * 128 + s * 64 + ar,
                                           (hA + 4) * 128 + s * 64 + ar]))
    wq_t = _tile_cols(Wq, wq_maps)
    wk_maps = []
    for g in range(2):
        for s in range(2):
            wk_maps.append(np.concatenate([(2 * g) * 128 + s * 64 + ar,
                                           (2 * g + 1) * 128 + s * 64 + ar]))
    wk_t = _tile_cols(Wk, wk_maps)
    c128 = np.arange(128)
    wo_t = _tile_cols(Wo, [j * 128 + c128 for j in range(16)])
    wqb_t = _tile_cols(np.ascontiguousarray(wqkv[:, :2048]),
                       [j * 128 + c128 for j in range(16)])
    wout_t = _tile_cols(woutb, [j * 128 + c128 for j in range(16)])
    wg_t = _tile_rowsT(wg)
    wu_t = _tile_rowsT(wu)
    wd_t = _tile_rowsT(wd)

    pmk_bd = np.zeros((128, 16, 32), np.float32)
    pmv_bd = np.zeros((32, 16, 128), np.float32)
    for j in range(16):
        pmk_bd[0:64, j, 0:16] = pm[0][2 * j].T
        pmk_bd[64:128, j, 16:32] = pm[0][2 * j + 1].T
        pmv_bd[0:16, j, 0:64] = pm[1][2 * j]
        pmv_bd[16:32, j, 64:128] = pm[1][2 * j + 1]

    pcol = np.arange(128)
    bmask = np.where((np.arange(512)[None, :] % 16) <= (pcol[:, None] % 16),
                     0.0, NEG).astype(np.float32)
    lam_bc = np.ascontiguousarray(np.broadcast_to(lam[None, :], (128, 16))).astype(np.float32)
    wn_fm = np.ascontiguousarray(wnorm.reshape(DC, 128).T)
    identity = np.eye(128, dtype=ml_dtypes.bfloat16)

    inv_freq = 1.0 / (10000.0 ** (np.arange(0, 128, 2, dtype=np.float64) / 128.0))
    shared = dict(wq_t=wq_t, wk_t=wk_t, wv=Wv, wo_t=wo_t, lam_bc=lam_bc,
                  wqb_t=wqb_t, wout_t=wout_t, pmk_bd=pmk_bd, pmv_bd=pmv_bd,
                  bmask=bmask, wn_fm=wn_fm, wg_t=wg_t, wu_t=wu_t, wd_t=wd_t,
                  ident=identity)

    in_maps = []
    for c in range(8):
        b, half = c // 2, c % 2
        T0 = half * NT
        xo = _fmify(X[b, T0:T0 + NT])
        if half == 1:
            xh = _fmify(X[b, T0 - HALO:T0])
        else:
            xh = np.zeros((128, DC, HALO), np.float32)
        pos = np.clip(np.arange(T0 - HALO, T0 + NT), 0, None).astype(np.float64)
        ang = pos[None, :] * inv_freq[:, None]
        cosb = np.concatenate([np.cos(ang), np.cos(ang)], axis=0).astype(np.float32)
        sinb = np.concatenate([np.sin(ang), np.sin(ang)], axis=0).astype(np.float32)
        qi_ = np.arange(8)[:, None, None]
        p_ = pcol[None, :, None]
        cix = np.arange(640)[None, None, :]
        qrow = T0 + qi_ * 128 + p_
        kpos = T0 + (qi_ + cix // 128) * 128 + (cix % 128) - HALO
        ok = (kpos >= 0) & (kpos <= qrow) & (kpos >= qrow - 511)
        am = np.where(ok, 0.0, NEG).astype(np.float32)
        amask = np.ascontiguousarray(am.transpose(1, 0, 2)).astype(ml_dtypes.bfloat16)
        m = dict(shared)
        m.update(xo=xo, xh=xh, cosb=cosb.astype(ml_dtypes.bfloat16),
                 sinb=sinb.astype(ml_dtypes.bfloat16), amask=amask)
        in_maps.append(m)

    nc = build_program(PHASES)
    res = run_bass_kernel_spmd(nc, in_maps, core_ids=list(range(8)))
    global LAST_RES
    LAST_RES = res
    out = np.empty((4, 2048, D), np.float32)
    for c in range(8):
        b, half = c // 2, c % 2
        y = np.asarray(res.results[c]["y"])
        out[b, half * NT:(half + 1) * NT] = y.transpose(2, 1, 0).reshape(NT, D)
    return out
